# version 10
# baseline (speedup 1.0000x reference)
import contextlib
import numpy as np

import concourse.bass as bass
import concourse.mybir as mybir
from concourse.bass_utils import run_bass_kernel_spmd

F32 = mybir.dt.float32
BF16 = mybir.dt.bfloat16
I32 = mybir.dt.int32
AF = mybir.ActivationFunctionType
ALU = mybir.AluOpType
AX = mybir.AxisListType

D = 2048
H = 16
HL = 2
DH = 128
HID = 8192
NDC = D // 128
NHC = HID // 128
SCALE = 1.0 / np.sqrt(128.0)
SCALE_HI = float(np.float32(SCALE))
SCALE_LO = float(np.float32(SCALE - SCALE_HI))


class Ctr:
    def __init__(self, nc, es, name, step=1):
        self.h = es.enter_context(nc.semaphore(name))
        self.n = 0
        self.step = step


def build(S, mt=None):
    OWN = S // 8
    TS = OWN
    NSUB = TS // 128
    MT = mt if mt is not None else min(512, TS)
    NMT = TS // MT
    NQT = S // MT
    NKC = S // 128
    NPAIR = NKC // 2
    NQC = MT // 128

    nc = bass.Bass("TRN2", target_bir_lowering=False)

    x = nc.dram_tensor("x", [S, D], F32, kind="ExternalInput").ap()
    wqkvt6 = nc.dram_tensor("wqkvt6", [6, 128, D], F32, kind="ExternalInput").ap()
    woutt = nc.dram_tensor("woutt", [16, 128, D], F32, kind="ExternalInput").ap()
    wgt = nc.dram_tensor("wgt", [NHC, 128, D], F32, kind="ExternalInput").ap()
    wut = nc.dram_tensor("wut", [NHC, 128, D], F32, kind="ExternalInput").ap()
    wdt = nc.dram_tensor("wdt", [HID, D], F32, kind="ExternalInput").ap()
    ident = nc.dram_tensor("ident", [128, 128], F32, kind="ExternalInput").ap()
    coff_in = nc.dram_tensor("coff", [1, 1], I32, kind="ExternalInput").ap()
    out = nc.dram_tensor("out", [OWN, D], F32, kind="ExternalOutput").ap()
    kth_h = nc.dram_tensor("kth_h", [HL * 128, S], BF16).ap()
    ktl_h = nc.dram_tensor("ktl_h", [HL * 128, S], BF16).ap()
    vT_h = nc.dram_tensor("vT_h", [HL * 128, S], F32).ap()
    qT_h = nc.dram_tensor("qT_h", [HL * 128, S], F32).ap()
    oT_loc = nc.dram_tensor("oT_loc", [HL * 128, S], F32)
    oT_loc_ap = oT_loc.ap()
    gout = nc.dram_tensor("gout", [8 * HL * 128, S], F32)
    gout_ap = gout.ap()
    xnh_h = nc.dram_tensor("xnh_h", [128, NDC * S // 8 * 8 // 8], BF16)
    del xnh_h
    xnh_hbm = nc.dram_tensor("xnh_hbm", [128, NDC * OWN], BF16).ap()
    xnl_hbm = nc.dram_tensor("xnl_hbm", [128, NDC * OWN], BF16).ap()

    es = contextlib.ExitStack()
    id_sb = es.enter_context(nc.sbuf_tensor("id_sb", [128, 128], F32))
    onescol = es.enter_context(nc.sbuf_tensor("onescol", [128, 1], F32))
    onesrow_bf = es.enter_context(nc.sbuf_tensor("onesrow_bf", [1, 128], BF16))
    onesrow32 = es.enter_context(nc.sbuf_tensor("onesrow32", [1, 128], F32))
    eps_sb = es.enter_context(nc.sbuf_tensor("eps_sb", [128, 1], F32))
    mean_sb = es.enter_context(nc.sbuf_tensor("mean_sb", [128, 1], F32))
    ex2_sb = es.enter_context(nc.sbuf_tensor("ex2_sb", [128, 1], F32))
    msq_sb = es.enter_context(nc.sbuf_tensor("msq_sb", [128, 1], F32))
    var_sb = es.enter_context(nc.sbuf_tensor("var_sb", [128, 1], F32))
    std_sb = es.enter_context(nc.sbuf_tensor("std_sb", [128, 1], F32))
    rstd_sb = es.enter_context(nc.sbuf_tensor("rstd_sb", [128, 1], F32))
    rtmp_sb = es.enter_context(nc.sbuf_tensor("rtmp_sb", [128, 1], F32))
    coff_sb = es.enter_context(nc.sbuf_tensor("coff_sb", [1, 1], I32))
    m_sb = es.enter_context(nc.sbuf_tensor("m_sb", [128, 4], F32))
    mneg_sb = es.enter_context(nc.sbuf_tensor("mneg_sb", [128, 4], F32))
    mrow_sb = es.enter_context(nc.sbuf_tensor("mrow_sb", [1, 512], BF16))
    rmax_sb = es.enter_context(nc.sbuf_tensor("rmax_sb", [128, 1], F32))

    psA = es.enter_context(nc.psum_tensor("psA", [128, 1024], F32))
    psB = es.enter_context(nc.psum_tensor("psB", [128, 1024], F32))
    psC = es.enter_context(nc.psum_tensor("psC", [128, 512], F32))
    psD = es.enter_context(nc.psum_tensor("psD", [128, 512], F32))
    psE = es.enter_context(nc.psum_tensor("psE", [128, 512], F32))
    psF = es.enter_context(nc.psum_tensor("psF", [128, 512], F32))

    tS = Ctr(nc, es, "tS")
    vS = Ctr(nc, es, "vS")
    sS = Ctr(nc, es, "sS")
    gS = Ctr(nc, es, "gS")
    cc = Ctr(nc, es, "cc")
    dX = [Ctr(nc, es, "dX0", 16), Ctr(nc, es, "dX1", 16)]
    dW = [Ctr(nc, es, "dW0", 16), Ctr(nc, es, "dW1", 16)]
    dU = [Ctr(nc, es, "dU0", 16), Ctr(nc, es, "dU1", 16)]
    dDm = [Ctr(nc, es, "dD0", 16), Ctr(nc, es, "dD1", 16)]
    dK = [Ctr(nc, es, "dK0", 16), Ctr(nc, es, "dK1", 16)]
    dV = Ctr(nc, es, "dV", 16)
    dQ = Ctr(nc, es, "dQ", 16)
    dO = Ctr(nc, es, "dO", 16)
    dG = [Ctr(nc, es, "dG0", 16), Ctr(nc, es, "dG1", 16)]
    dR = Ctr(nc, es, "dR", 16)
    ALL_DMA = dX + dW + dU + dDm + dK + [dV, dQ, dO, dR] + dG

    prog = {k: [] for k in ("tensor", "vector", "scalar", "sync", "gpsimd")}
    hw = {}

    def em(e, f):
        prog[e].append(f)

    def wait(e, ctr, n=None):
        v = ctr.n if n is None else n
        if v <= 0:
            return
        key = (e, ctr.h.name)
        if hw.get(key, 0) >= v:
            return
        hw[key] = v
        em(e, lambda E, h=ctr.h, v=v: E.wait_ge(h, v))

    def ins(e, f, ctr=None):
        if ctr is None:
            em(e, lambda E, f=f: f(E))
            return None
        ctr.n += ctr.step
        em(e, lambda E, f=f, h=ctr.h, s=ctr.step: f(E).then_inc(h, s))
        return ctr.n

    def barrier():
        cts = [tS, vS, sS, gS] + ALL_DMA
        for e in prog:
            for c in cts:
                wait(e, c)

    def mm3(ps_ap_f, lh, ll, rh, rl, start, stop, ctr):
        ins("tensor", lambda E, a=ps_ap_f, b=lh, c=rh, st=start: E.matmul(
            a(), b(), c(), start=st, stop=False), None)
        ins("tensor", lambda E, a=ps_ap_f, b=lh, c=rl: E.matmul(
            a(), b(), c(), start=False, stop=False), None)
        return ins("tensor", lambda E, a=ps_ap_f, b=ll, c=rh, sp=stop: E.matmul(
            a(), b(), c(), start=False, stop=sp), ctr)

    ins("sync", lambda E: E.dma_start(out=id_sb[:], in_=ident), dO)
    ins("sync", lambda E: E.dma_start(out=coff_sb[:], in_=coff_in), dO)
    dynregs = {}

    def dyninit(E):
        dynregs["coff"] = E.alloc_register("r_coff")
        dynregs["lin"] = E.alloc_register("r_lin")
        dynregs["row"] = E.alloc_register("r_row")
        dynregs["off"] = E.alloc_register("r_off")
        E.reg_load(dynregs["coff"], coff_sb[:1, :1])
    wait("gpsimd", dO)
    ins("gpsimd", dyninit)
    ins("gpsimd", lambda E: E.memset(onescol[:], 1.0), gS)
    ins("gpsimd", lambda E: E.memset(onesrow_bf[:], 1.0), gS)
    ins("gpsimd", lambda E: E.memset(onesrow32[:], 1.0), gS)
    ins("gpsimd", lambda E: E.memset(eps_sb[:], 1e-5), gS)
    barrier()

    es12 = contextlib.ExitStack()
    xnh = es12.enter_context(nc.sbuf_tensor("xnh", [128, NDC * TS], BF16))
    xnl = es12.enter_context(nc.sbuf_tensor("xnl", [128, NDC * TS], BF16))
    xrow = [es12.enter_context(nc.sbuf_tensor(f"xrow{i}", [128, D], F32)) for i in range(2)]
    sq_sb = es12.enter_context(nc.sbuf_tensor("sq_sb", [128, D], F32))
    wf_sb = [es12.enter_context(nc.sbuf_tensor(f"wf{i}", [128, D], F32)) for i in range(2)]
    wh_sb = [es12.enter_context(nc.sbuf_tensor(f"wh{i}", [128, D], BF16)) for i in range(2)]
    wl_sb = [es12.enter_context(nc.sbuf_tensor(f"wl{i}", [128, D], BF16)) for i in range(2)]
    evh_sb = [es12.enter_context(nc.sbuf_tensor(f"evh{i}", [128, 512], BF16)) for i in range(2)]
    evl_sb = [es12.enter_context(nc.sbuf_tensor(f"evl{i}", [128, 512], BF16)) for i in range(2)]
    ev32_sb = [es12.enter_context(nc.sbuf_tensor(f"ev32{i}", [128, 512], F32)) for i in range(2)]

    last_xrow_reader = [0, 0]
    last_evac_store = [0, 0]
    last_wslab_reader = [0, 0]
    last_psT_rd_v = [0, 0]
    last_psT_rd_s = [0, 0]
    last_psP_rd = [0, 0]
    last_proj_mm_prev = 0
    last_psP_rd_s2 = [0, 0]
    last_split_wv = 0
    xq_evac = {}

    supertiles = list(range(1, 8)) + [0]
    for t_i, t in enumerate(supertiles):
        for j in range(NSUB):
            sl = j % 2
            wait("sync", tS, last_xrow_reader[sl])
            ld = ins("sync", lambda E, t=t, j=j, sl=sl: E.dma_start(
                out=xrow[sl][:], in_=x[t * TS + 128 * j: t * TS + 128 * (j + 1), :]), dX[sl])
            wait("vector", dX[sl], ld)
            n1 = ins("vector", lambda E, sl=sl: E.reduce_sum(mean_sb[:], xrow[sl][:], axis=AX.X), vS)
            wait("vector", vS, n1)
            n2 = ins("vector", lambda E: E.tensor_scalar_mul(mean_sb[:], mean_sb[:], 1.0 / D), vS)
            wait("scalar", dX[sl], ld)
            m1 = ins("scalar", lambda E, sl=sl: E.activation(sq_sb[:], xrow[sl][:], AF.Square), sS)
            wait("vector", sS, m1)
            n3 = ins("vector", lambda E: E.reduce_sum(ex2_sb[:], sq_sb[:], axis=AX.X), vS)
            wait("vector", vS, n3)
            n4 = ins("vector", lambda E: E.tensor_scalar_mul(ex2_sb[:], ex2_sb[:], 1.0 / D), vS)
            wait("vector", vS, n4)
            n5 = ins("vector", lambda E: E.tensor_tensor(msq_sb[:], mean_sb[:], mean_sb[:], ALU.mult), vS)
            wait("vector", vS, n5)
            n6 = ins("vector", lambda E: E.tensor_tensor(var_sb[:], ex2_sb[:], msq_sb[:], ALU.subtract), vS)
            wait("scalar", vS, n6)
            m2 = ins("scalar", lambda E: E.activation(std_sb[:], var_sb[:], AF.Sqrt, bias=eps_sb[:]), sS)
            wait("vector", sS, m2)
            n7 = ins("vector", lambda E: E.reciprocal(rstd_sb[:], std_sb[:]), vS)
            wait("vector", vS, n7)
            n7b = ins("vector", lambda E: E.tensor_tensor(rtmp_sb[:], std_sb[:], rstd_sb[:], ALU.mult), vS)
            wait("vector", vS, n7b)
            n7c = ins("vector", lambda E: E.tensor_scalar(rtmp_sb[:], rtmp_sb[:], -1.0, 2.0, ALU.mult, ALU.add), vS)
            wait("vector", vS, n7c)
            n7d = ins("vector", lambda E: E.tensor_tensor(rstd_sb[:], rstd_sb[:], rtmp_sb[:], ALU.mult), vS)
            wait("vector", vS, n7d)
            n8 = ins("vector", lambda E, sl=sl: E.tensor_scalar(
                xrow[sl][:], xrow[sl][:], mean_sb[:], rstd_sb[:], ALU.subtract, ALU.mult), vS)
            wait("tensor", vS, n8)
            for ci in range(NDC):
                bk = ci % 2
                pst = psC if bk == 0 else psD
                wait("tensor", vS, last_psT_rd_v[bk])
                wait("tensor", sS, last_psT_rd_s[bk])
                t1 = ins("tensor", lambda E, sl=sl, ci=ci, pst=pst: E.transpose(
                    pst[:, 0:128], xrow[sl][:, ci * 128:(ci + 1) * 128], id_sb[:]), tS)
                wait("scalar", tS, t1)
                if j == 0 and ci == 0 and t_i > 0:
                    wait("scalar", tS, last_proj_mm_prev)
                    wait("vector", tS, last_proj_mm_prev)
                e1 = ins("scalar", lambda E, pst=pst, ci=ci, j=j: E.activation(
                    xnh[:, ci * TS + 128 * j: ci * TS + 128 * (j + 1)], pst[:, 0:128], AF.Copy), sS)
                last_psT_rd_s[bk] = e1
                wait("vector", sS, e1)
                e2 = ins("vector", lambda E, pst=pst, ci=ci, j=j: E.scalar_tensor_tensor(
                    xnl[:, ci * TS + 128 * j: ci * TS + 128 * (j + 1)], pst[:, 0:128], 1.0,
                    xnh[:, ci * TS + 128 * j: ci * TS + 128 * (j + 1)], ALU.mult, ALU.subtract), vS)
                last_psT_rd_v[bk] = e2
                if ci == NDC - 1:
                    last_xrow_reader[sl] = t1
        a_ready = vS.n

        dests = [(kth_h, ktl_h, 0), (kth_h, ktl_h, 128), (vT_h, None, 0), (vT_h, None, 128),
                 (qT_h, None, 0), (qT_h, None, 128)]
        for g in range(6):
            wsl = g % 2
            dh, dl, rbase = dests[g]
            wait("sync", tS, last_wslab_reader[wsl])
            wait("sync", vS, last_split_wv)
            wl_ld = ins("sync", lambda E, g=g, wsl=wsl: E.dma_start(
                out=wf_sb[wsl][:], in_=wqkvt6[g:g + 1].rearrange("o p d -> (o p) d")), dW[wsl])
            wait("vector", dW[wsl], wl_ld)
            wait("vector", tS, last_wslab_reader[wsl])
            s1 = ins("vector", lambda E, wsl=wsl: E.tensor_copy(wh_sb[wsl][:], wf_sb[wsl][:]), vS)
            wait("vector", vS, s1)
            s2 = ins("vector", lambda E, wsl=wsl: E.tensor_tensor(
                wl_sb[wsl][:], wf_sb[wsl][:], wh_sb[wsl][:], ALU.subtract), vS)
            last_split_wv = s2
            for mv in range(NMT):
                pb = (g * NMT + mv) % 2
                psP = psE if pb == 0 else psF
                wait("tensor", vS, s2)
                if g == 0 and mv == 0:
                    wait("tensor", vS, a_ready)
                    wait("tensor", sS, sS.n)
                wait("tensor", vS, last_psP_rd[pb])
                wait("tensor", sS, last_psP_rd_s2[pb])
                for ci in range(NDC):
                    mm3(lambda ci=ci, psP=psP: psP[:, 0:MT],
                        lambda wsl=wsl, ci=ci: wh_sb[wsl][:, ci * 128:(ci + 1) * 128],
                        lambda wsl=wsl, ci=ci: wl_sb[wsl][:, ci * 128:(ci + 1) * 128],
                        lambda ci=ci, mv=mv: xnh[:, ci * TS + mv * MT: ci * TS + mv * MT + MT],
                        lambda ci=ci, mv=mv: xnl[:, ci * TS + mv * MT: ci * TS + mv * MT + MT],
                        start=(ci == 0), stop=(ci == NDC - 1),
                        ctr=tS if ci == NDC - 1 else None)
                mmn = tS.n
                last_wslab_reader[wsl] = mmn
                evsl = pb
                if dl is not None:
                    wait("scalar", tS, mmn)
                    wait("scalar", dG[evsl], last_evac_store[evsl])
                    eh = ins("scalar", lambda E, psP=psP, evsl=evsl: E.activation(
                        evh_sb[evsl][:, 0:MT], psP[:, 0:MT], AF.Copy), sS)
                    wait("vector", sS, eh)
                    wait("vector", dG[evsl], last_evac_store[evsl])
                    el = ins("vector", lambda E, psP=psP, evsl=evsl: E.scalar_tensor_tensor(
                        evl_sb[evsl][:, 0:MT], psP[:, 0:MT], 1.0, evh_sb[evsl][:, 0:MT],
                        ALU.mult, ALU.subtract), vS)
                    last_psP_rd[pb] = el
                    last_psP_rd_s2[pb] = eh
                    wait("gpsimd", vS, el)
                    st1 = ins("gpsimd", lambda E, dh=dh, rbase=rbase, t=t, mv=mv, evsl=evsl: E.dma_start(
                        out=dh[rbase:rbase + 128, t * TS + mv * MT: t * TS + mv * MT + MT],
                        in_=evh_sb[evsl][:, 0:MT]), dG[evsl])
                    st2 = ins("gpsimd", lambda E, dl=dl, rbase=rbase, t=t, mv=mv, evsl=evsl: E.dma_start(
                        out=dl[rbase:rbase + 128, t * TS + mv * MT: t * TS + mv * MT + MT],
                        in_=evl_sb[evsl][:, 0:MT]), dG[evsl])
                    last_evac_store[evsl] = st2
                else:
                    wait("vector", tS, mmn)
                    wait("vector", dG[evsl], last_evac_store[evsl])
                    ev = ins("vector", lambda E, psP=psP, evsl=evsl: E.tensor_copy(
                        ev32_sb[evsl][:, 0:MT], psP[:, 0:MT]), vS)
                    last_psP_rd[pb] = ev
                    wait("gpsimd", vS, ev)
                    st1 = ins("gpsimd", lambda E, dh=dh, rbase=rbase, t=t, mv=mv, evsl=evsl: E.dma_start(
                        out=dh[rbase:rbase + 128, t * TS + mv * MT: t * TS + mv * MT + MT],
                        in_=ev32_sb[evsl][:, 0:MT]), dG[evsl])
                    last_evac_store[evsl] = st1
        last_proj_mm_prev = tS.n

    wait("gpsimd", vS)
    wait("gpsimd", tS)
    wait("gpsimd", dG[0])
    wait("gpsimd", dG[1])
    ins("gpsimd", lambda E: E.dma_start(out=xnh_hbm, in_=xnh[:, :]), dG[0])
    ins("gpsimd", lambda E: E.dma_start(out=xnl_hbm, in_=xnl[:, :]), dG[1])
    barrier()
    es12.close()

    es3 = contextlib.ExitStack()
    kth_sb = es3.enter_context(nc.sbuf_tensor("kth_sb", [128, 2 * S], BF16))
    ktl_sb = es3.enter_context(nc.sbuf_tensor("ktl_sb", [128, 2 * S], BF16))
    v32_sb = es3.enter_context(nc.sbuf_tensor("v32_sb", [128, S], F32))
    vh_sb = es3.enter_context(nc.sbuf_tensor("vh_sb", [128, S], BF16))
    vl_sb = es3.enter_context(nc.sbuf_tensor("vl_sb", [128, S], BF16))
    qsh_sb = es3.enter_context(nc.sbuf_tensor("qsh_sb", [128, S], BF16))
    qsl_sb = es3.enter_context(nc.sbuf_tensor("qsl_sb", [128, S], BF16))
    q32_sb = es3.enter_context(nc.sbuf_tensor("q32_sb", [128, 1024], F32))
    qt32_sb = es3.enter_context(nc.sbuf_tensor("qt32_sb", [128, 1024], F32))
    p32_sb = [es3.enter_context(nc.sbuf_tensor(f"p32_{i}", [128, 1024], F32)) for i in range(2)]
    ph_sb = [es3.enter_context(nc.sbuf_tensor(f"ph_{i}", [128, 1024], BF16)) for i in range(2)]
    pl_sb = [es3.enter_context(nc.sbuf_tensor(f"pl_{i}", [128, 1024], BF16)) for i in range(2)]
    z_sb = es3.enter_context(nc.sbuf_tensor("z_sb", [128, 1024], F32))
    zb_sb = es3.enter_context(nc.sbuf_tensor("zb_sb", [128, 512], F32))
    zr_sb = es3.enter_context(nc.sbuf_tensor("zr_sb", [1, 512], F32))
    zt_sb = es3.enter_context(nc.sbuf_tensor("zt_sb", [1, 512], F32))
    ot_sb = es3.enter_context(nc.sbuf_tensor("ot_sb", [128, OWN], F32))

    prev_head_tS = 0
    prev_head_vS = 0
    prev_head_sS = 0
    prev_ot_store = 0
    exps = []
    zs = []
    avs = []
    p1_reads = []
    psef_rd = [0, 0]
    p1_i = 0
    ot_read = 0
    last_q32_rd = 0
    gsp2s = []
    gzs = []
    last_zfin = 0

    for hl in range(HL):
        half = hl % 2
        wait("sync", tS, prev_head_tS)
        klh = ins("sync", lambda E, hl=hl, half=half: E.dma_start(
            out=kth_sb[:, half * S:(half + 1) * S], in_=kth_h[hl * 128:(hl + 1) * 128, :]), dK[half])
        kll = ins("sync", lambda E, hl=hl, half=half: E.dma_start(
            out=ktl_sb[:, half * S:(half + 1) * S], in_=ktl_h[hl * 128:(hl + 1) * 128, :]), dK[half])
        wait("sync", vS, prev_head_vS)
        wait("sync", sS, prev_head_sS)
        wait("sync", dG[0], prev_ot_store)
        vl32 = ins("sync", lambda E, hl=hl: E.dma_start(
            out=v32_sb[:], in_=vT_h[hl * 128:(hl + 1) * 128, :]), dV)

        wait("tensor", dV, vl32)
        wait("tensor", vS, prev_head_vS)
        wait("tensor", sS, prev_head_sS)
        for kc in range(NKC):
            bk = kc % 2
            pst = psC if bk == 0 else psD
            wait("tensor", vS, last_psT_rd_v[bk])
            wait("tensor", sS, last_psT_rd_s[bk])
            t1 = ins("tensor", lambda E, kc=kc, pst=pst: E.transpose(
                pst[:, 0:128], v32_sb[:, kc * 128:(kc + 1) * 128], id_sb[:]), tS)
            wait("scalar", tS, t1)
            e1 = ins("scalar", lambda E, kc=kc, pst=pst: E.activation(
                vh_sb[:, kc * 128:(kc + 1) * 128], pst[:, 0:128], AF.Copy), sS)
            last_psT_rd_s[bk] = e1
            wait("vector", sS, e1)
            e2 = ins("vector", lambda E, kc=kc, pst=pst: E.scalar_tensor_tensor(
                vl_sb[:, kc * 128:(kc + 1) * 128], pst[:, 0:128], 1.0,
                vh_sb[:, kc * 128:(kc + 1) * 128], ALU.mult, ALU.subtract), vS)
            last_psT_rd_v[bk] = e2
        v_done = vS.n

        wait("vector", tS, prev_head_tS)
        for qc8 in range(S // 1024):
            wait("sync", vS, last_q32_rd)
            ql = ins("sync", lambda E, hl=hl, qc8=qc8: E.dma_start(
                out=q32_sb[:], in_=qT_h[hl * 128:(hl + 1) * 128, qc8 * 1024:(qc8 + 1) * 1024]), dQ)
            wait("vector", dQ, ql)
            c1 = ins("vector", lambda E: E.tensor_scalar_mul(qt32_sb[:], q32_sb[:], SCALE_LO), vS)
            wait("vector", vS, c1)
            c2 = ins("vector", lambda E: E.tensor_scalar_mul(q32_sb[:], q32_sb[:], SCALE_HI), vS)
            wait("vector", vS, c2)
            c3 = ins("vector", lambda E: E.tensor_tensor(q32_sb[:], q32_sb[:], qt32_sb[:], ALU.add), vS)
            wait("vector", vS, c3)
            c4 = ins("vector", lambda E, qc8=qc8: E.tensor_copy(
                qsh_sb[:, qc8 * 1024:(qc8 + 1) * 1024], q32_sb[:]), vS)
            wait("vector", vS, c4)
            c5 = ins("vector", lambda E, qc8=qc8: E.tensor_tensor(
                qsl_sb[:, qc8 * 1024:(qc8 + 1) * 1024], q32_sb[:],
                qsh_sb[:, qc8 * 1024:(qc8 + 1) * 1024], ALU.subtract), vS)
            last_q32_rd = c5
        q_ready = vS.n

        mrow_done_d = {}
        for qt in range(NQT):
            for qc in range(NQC):
                qg = qt * NQC + qc
                for kt in range(S // 512):
                    pb = p1_i % 2
                    p1_i += 1
                    psP1 = psE if pb == 0 else psF
                    wait("tensor", dK[half], kll)
                    wait("tensor", vS, q_ready)
                    wait("tensor", vS, psef_rd[pb])
                    pm = ins("tensor", lambda E, qg=qg, kt=kt, half=half, psP1=psP1: E.matmul(
                        psP1[:, 0:512],
                        qsh_sb[:, qg * 128:(qg + 1) * 128],
                        kth_sb[:, half * S + kt * 512: half * S + (kt + 1) * 512],
                        start=True, stop=True), tS)
                    wait("vector", tS, pm)
                    if p1_reads:
                        wait("vector", vS, p1_reads[-1])
                    r1 = ins("vector", lambda E, psP1=psP1: E.reduce_max(
                        rmax_sb[:, 0:1], psP1[:, 0:512], axis=AX.X), vS)
                    psef_rd[pb] = r1
                    wait("vector", vS, r1)
                    if kt == 0:
                        r2 = ins("vector", lambda E, qc=qc: E.tensor_copy(m_sb[:, qc:qc + 1], rmax_sb[:, 0:1]), vS)
                    else:
                        r2 = ins("vector", lambda E, qc=qc: E.tensor_tensor(
                            m_sb[:, qc:qc + 1], m_sb[:, qc:qc + 1], rmax_sb[:, 0:1], ALU.max), vS)
                    p1_reads.append(r2)
                wait("vector", vS, p1_reads[-1])
                ng = ins("vector", lambda E, qc=qc: E.tensor_scalar_mul(
                    mneg_sb[:, qc:qc + 1], m_sb[:, qc:qc + 1], -1.0), vS)
                wait("tensor", vS, ng)
                tm = ins("tensor", lambda E, qc=qc: E.matmul(
                    psD[0:1, 0:128], mneg_sb[:, qc:qc + 1], id_sb[:], start=True, stop=True), tS)
                wait("vector", tS, tm)
                mc = ins("vector", lambda E, qc=qc: E.tensor_copy(
                    mrow_sb[0:1, qc * 128:(qc + 1) * 128], psD[0:1, 0:128]), vS)
                p1_reads.append(mc)
            mrow_done_d[qt] = vS.n

            smms = []
            _ = None

            def emit_S(pr, qt=qt, half=half):
                kc0, kc1 = 2 * pr, 2 * pr + 1
                fp = len(smms)
                sb = fp % 2
                psS = psA if sb == 0 else psB
                if fp >= 2:
                    wait("tensor", sS, exps[fp - 2])
                if pr == 0:
                    wait("tensor", vS, mrow_done_d[qt])
                for half_i, kc in enumerate((kc0, kc1)):
                    mm3(lambda psS=psS, half_i=half_i: psS[:, half_i * MT:(half_i + 1) * MT],
                        lambda kc=kc: kth_sb[:, half * S + kc * 128: half * S + (kc + 1) * 128],
                        lambda kc=kc: ktl_sb[:, half * S + kc * 128: half * S + (kc + 1) * 128],
                        lambda: qsh_sb[:, qt * MT:(qt + 1) * MT],
                        lambda: qsl_sb[:, qt * MT:(qt + 1) * MT],
                        start=True, stop=False, ctr=None)
                    last = (half_i == 1)
                    ins("tensor", lambda E, psS=psS, half_i=half_i: E.matmul(
                        psS[:, half_i * MT:(half_i + 1) * MT], onesrow_bf[:],
                        mrow_sb[0:1, 0:MT],
                        start=False, stop=True), tS if last else None)
                smms.append(tS.n)

            emit_S(0)
            for pr in range(NPAIR):
                kc0, kc1 = 2 * pr, 2 * pr + 1
                fp = len(exps)
                sb = fp % 2
                psS = psA if sb == 0 else psB
                if pr + 1 < NPAIR:
                    emit_S(pr + 1)
                wait("scalar", tS, smms[pr])
                if fp >= 2:
                    wait("scalar", vS, gsp2s[fp - 2])
                    wait("scalar", gS, gzs[fp - 2])
                ec = ins("scalar", lambda E, sb=sb, psS=psS, MT=MT: E.activation(
                    p32_sb[sb][:, 0:2 * MT], psS[:, 0:2 * MT], AF.Exp), sS)
                exps.append(ec)
                wait("vector", sS, ec)
                if fp >= 2:
                    wait("vector", tS, avs[fp - 2])
                sp1 = ins("vector", lambda E, sb=sb, MT=MT: E.tensor_copy(
                    ph_sb[sb][:, 0:2 * MT], p32_sb[sb][:, 0:2 * MT]), vS)
                wait("vector", vS, sp1)
                sp2 = ins("vector", lambda E, sb=sb, MT=MT: E.tensor_tensor(
                    pl_sb[sb][:, 0:2 * MT], p32_sb[sb][:, 0:2 * MT], ph_sb[sb][:, 0:2 * MT],
                    ALU.subtract), vS)
                gsp2s.append(sp2)
                zs.append(sp2)
                wait("gpsimd", sS, ec)
                if pr == 0:
                    wait("gpsimd", tS, last_zfin)
                    zc0 = ins("gpsimd", lambda E, sb=sb, MT=MT: E.tensor_copy(
                        z_sb[:, 0:2 * MT], p32_sb[sb][:, 0:2 * MT]), gS)
                else:
                    wait("gpsimd", gS, gzs[-1])
                    zc0 = ins("gpsimd", lambda E, sb=sb, MT=MT: E.tensor_tensor(
                        z_sb[:, 0:2 * MT], z_sb[:, 0:2 * MT], p32_sb[sb][:, 0:2 * MT], ALU.add), gS)
                gzs.append(zc0)
                wait("tensor", vS, sp1)
                if pr == 0:
                    wait("tensor", vS, ot_read)
                for half_i, kc in enumerate((kc0, kc1)):
                    ins("tensor", lambda E, kc=kc, sb=sb, half_i=half_i, pr=pr: E.matmul(
                        psC[:, 0:MT], vh_sb[:, kc * 128:(kc + 1) * 128],
                        ph_sb[sb][:, half_i * MT:(half_i + 1) * MT],
                        start=(pr == 0 and half_i == 0), stop=False), None)
                    ins("tensor", lambda E, kc=kc, sb=sb, half_i=half_i: E.matmul(
                        psC[:, 0:MT], vl_sb[:, kc * 128:(kc + 1) * 128],
                        ph_sb[sb][:, half_i * MT:(half_i + 1) * MT],
                        start=False, stop=False), None)
                wait("tensor", vS, sp2)
                for half_i, kc in enumerate((kc0, kc1)):
                    last = (half_i == 1)
                    ins("tensor", lambda E, kc=kc, sb=sb, half_i=half_i, pr=pr: E.matmul(
                        psC[:, 0:MT], vh_sb[:, kc * 128:(kc + 1) * 128],
                        pl_sb[sb][:, half_i * MT:(half_i + 1) * MT],
                        start=False, stop=(pr == NPAIR - 1 and half_i == 1)),
                        tS if last else None)
                avs.append(tS.n)
            av_last = avs[-1]
            wait("tensor", gS, gzs[-1])
            ins("tensor", lambda E: E.matmul(psD[0:1, 0:MT], onescol[:], z_sb[:, 0:MT],
                                             start=True, stop=False), None)
            zf = ins("tensor", lambda E: E.matmul(psD[0:1, 0:MT], onescol[:], z_sb[:, MT:2 * MT],
                                                  start=False, stop=True), tS)
            last_zfin = zf
            wait("vector", tS, zf)
            zr1 = ins("vector", lambda E: E.reciprocal(zr_sb[0:1, 0:MT], psD[0:1, 0:MT]), vS)
            wait("vector", vS, zr1)
            zr2 = ins("vector", lambda E: E.tensor_tensor(zt_sb[0:1, 0:MT], psD[0:1, 0:MT], zr_sb[0:1, 0:MT], ALU.mult), vS)
            wait("vector", vS, zr2)
            zr3 = ins("vector", lambda E: E.tensor_scalar(zt_sb[0:1, 0:MT], zt_sb[0:1, 0:MT], -1.0, 2.0, ALU.mult, ALU.add), vS)
            wait("vector", vS, zr3)
            zr4 = ins("vector", lambda E: E.tensor_tensor(zr_sb[0:1, 0:MT], zr_sb[0:1, 0:MT], zt_sb[0:1, 0:MT], ALU.mult), vS)
            wait("tensor", vS, zr4)
            wait("tensor", vS, psef_rd[0])
            zb1 = ins("tensor", lambda E: E.matmul(psE[:, 0:MT], onesrow32[:], zr_sb[0:1, 0:MT], start=True, stop=True), tS)
            wait("vector", tS, zb1)
            wait("vector", vS, ot_read)
            zb2 = ins("vector", lambda E: E.tensor_copy(zb_sb[:, 0:MT], psE[:, 0:MT]), vS)
            psef_rd[0] = zb2
            wait("vector", tS, av_last)
            wait("vector", vS, zb2)
            obase = (qt % (OWN // MT)) * MT
            if obase == 0:
                wait("vector", dG[0], prev_ot_store)
            oc_ = ins("vector", lambda E, obase=obase: E.tensor_tensor(
                ot_sb[:, obase:obase + MT], psC[:, 0:MT], zb_sb[:, 0:MT], ALU.mult), vS)
            ot_read = oc_
            if (qt + 1) % (OWN // MT) == 0:
                wait("gpsimd", vS, oc_)
                def dynstore(E, hl=hl, qt=qt):
                    if qt + 1 == OWN // MT:
                        if hl == 0:
                            E.reg_mov(dynregs["row"], 0)
                        else:
                            E.reg_add(dynregs["row"], dynregs["row"], 128 * S)
                        E.reg_mov(dynregs["lin"], dynregs["coff"])
                    E.reg_mod(dynregs["off"], dynregs["lin"], S)
                    E.reg_add(dynregs["off"], dynregs["off"], dynregs["row"])
                    r = E.dma_start(out=bass.AP(oT_loc, dynregs["off"], [[S, 128], [1, OWN]]),
                                    in_=ot_sb[:, 0:OWN])
                    E.reg_add(dynregs["lin"], dynregs["lin"], OWN)
                    return r
                st = ins("gpsimd", dynstore, dG[0])
                prev_ot_store = st
        prev_head_tS = tS.n
        prev_head_vS = vS.n
        prev_head_sS = sS.n

    wait("gpsimd", dG[0])
    ins("gpsimd", lambda E: E.collective_compute(
        "AllGather", ALU.bypass, replica_groups=[[0, 1, 2, 3, 4, 5, 6, 7]],
        ins=[oT_loc_ap], outs=[gout_ap]), cc)
    barrier()
    es3.close()

    es5 = contextlib.ExitStack()
    acc = es5.enter_context(nc.sbuf_tensor("acc", [128, NDC * OWN], F32))
    wstage = [es5.enter_context(nc.sbuf_tensor(f"wst{i}", [128, D], F32)) for i in range(2)]
    esF = contextlib.ExitStack()
    xnh2 = esF.enter_context(nc.sbuf_tensor("xnh2", [128, NDC * OWN], BF16))
    xnl2 = esF.enter_context(nc.sbuf_tensor("xnl2", [128, NDC * OWN], BF16))
    wgh = [esF.enter_context(nc.sbuf_tensor(f"wgh{i}", [128, D], BF16)) for i in range(2)]
    wgl = [esF.enter_context(nc.sbuf_tensor(f"wgl{i}", [128, D], BF16)) for i in range(2)]
    wuh = [esF.enter_context(nc.sbuf_tensor(f"wuh{i}", [128, D], BF16)) for i in range(2)]
    wul = [esF.enter_context(nc.sbuf_tensor(f"wul{i}", [128, D], BF16)) for i in range(2)]
    wdh = [esF.enter_context(nc.sbuf_tensor(f"wdh{i}", [128, D], BF16)) for i in range(2)]
    wdl = [esF.enter_context(nc.sbuf_tensor(f"wdl{i}", [128, D], BF16)) for i in range(2)]
    h32_sb = esF.enter_context(nc.sbuf_tensor("h32_sb", [128, 512], F32))
    HGRP = 2
    hh_sb = esF.enter_context(nc.sbuf_tensor("hh_sb", [128, HGRP * TS], BF16))
    hlo_sb = esF.enter_context(nc.sbuf_tensor("hlo_sb", [128, HGRP * TS], BF16))
    g_sb = esF.enter_context(nc.sbuf_tensor("g_sb", [128, 512], F32))

    ins("sync", lambda E: E.dma_start(out=xnh2[:, :], in_=xnh_hbm), dO)
    xn2_ld = ins("sync", lambda E: E.dma_start(out=xnl2[:, :], in_=xnl_hbm), dO)

    last_wg_rd = [0, 0]
    last_wu_rd = [0, 0]
    last_wd_rd = [0, 0]
    last_psd_rd = [0, 0]
    last_stage_rd = [0, 0]
    acc_written = {}
    last_hm = 0
    last_h_rd = {}
    last_g_rd = 0

    h_ready = {}
    h_rd = 0
    for hc in range(NHC):
        wsl = hc % 2
        hg = hc % HGRP
        stg = wsl
        wait("sync", vS, last_stage_rd[stg])
        wgld = ins("sync", lambda E, hc=hc, stg=stg: E.dma_start(
            out=wstage[stg][:], in_=wgt[hc:hc + 1].rearrange("o p d -> (o p) d")), dW[stg])
        wait("vector", dW[stg], wgld)
        wait("vector", tS, last_wg_rd[wsl])
        g1 = ins("vector", lambda E, stg=stg, wsl=wsl: E.tensor_copy(wgh[wsl][:], wstage[stg][:]), vS)
        wait("vector", vS, g1)
        g2 = ins("vector", lambda E, stg=stg, wsl=wsl: E.tensor_tensor(
            wgl[wsl][:], wstage[stg][:], wgh[wsl][:], ALU.subtract), vS)
        last_stage_rd[stg] = g2
        wait("sync", vS, g2)
        wuld = ins("sync", lambda E, hc=hc, stg=stg: E.dma_start(
            out=wstage[stg][:], in_=wut[hc:hc + 1].rearrange("o p d -> (o p) d")), dU[stg])
        wait("vector", dU[stg], wuld)
        wait("vector", tS, last_wu_rd[wsl])
        u1 = ins("vector", lambda E, stg=stg, wsl=wsl: E.tensor_copy(wuh[wsl][:], wstage[stg][:]), vS)
        wait("vector", vS, u1)
        u2 = ins("vector", lambda E, stg=stg, wsl=wsl: E.tensor_tensor(
            wul[wsl][:], wstage[stg][:], wuh[wsl][:], ALU.subtract), vS)
        last_stage_rd[stg] = u2
        wait("sync", vS, u2)
        wdld = ins("sync", lambda E, hc=hc, stg=stg: E.dma_start(
            out=wstage[stg][:], in_=wdt[hc * 128:(hc + 1) * 128, :]), dDm[stg])
        wait("vector", dDm[stg], wdld)
        wait("vector", tS, last_wd_rd[wsl])
        d1 = ins("vector", lambda E, stg=stg, wsl=wsl: E.tensor_copy(wdh[wsl][:], wstage[stg][:]), vS)
        wait("vector", vS, d1)
        d2 = ins("vector", lambda E, stg=stg, wsl=wsl: E.tensor_tensor(
            wdl[wsl][:], wstage[stg][:], wdh[wsl][:], ALU.subtract), vS)
        last_stage_rd[stg] = d2

        for mv in range(NMT):
            wait("tensor", vS, u2)
            if hc == 0 and mv == 0:
                wait("tensor", dO, xn2_ld)
            wait("tensor", sS, sS.n)
            wait("tensor", vS, last_hm)
            for ci in range(NDC):
                mm3(lambda: psA[:, 0:MT],
                    lambda wsl=wsl, ci=ci: wgh[wsl][:, ci * 128:(ci + 1) * 128],
                    lambda wsl=wsl, ci=ci: wgl[wsl][:, ci * 128:(ci + 1) * 128],
                    lambda ci=ci, mv=mv: xnh2[:, ci * TS + mv * MT: ci * TS + mv * MT + MT],
                    lambda ci=ci, mv=mv: xnl2[:, ci * TS + mv * MT: ci * TS + mv * MT + MT],
                    start=(ci == 0), stop=(ci == NDC - 1), ctr=tS if ci == NDC - 1 else None)
            gmm = tS.n
            last_wg_rd[wsl] = gmm
            for ci in range(NDC):
                mm3(lambda: psA[:, 512:512 + MT],
                    lambda wsl=wsl, ci=ci: wuh[wsl][:, ci * 128:(ci + 1) * 128],
                    lambda wsl=wsl, ci=ci: wul[wsl][:, ci * 128:(ci + 1) * 128],
                    lambda ci=ci, mv=mv: xnh2[:, ci * TS + mv * MT: ci * TS + mv * MT + MT],
                    lambda ci=ci, mv=mv: xnl2[:, ci * TS + mv * MT: ci * TS + mv * MT + MT],
                    start=(ci == 0), stop=(ci == NDC - 1), ctr=tS if ci == NDC - 1 else None)
            umm = tS.n
            last_wu_rd[wsl] = umm
            wait("scalar", tS, gmm)
            wait("scalar", vS, last_g_rd)
            sg = ins("scalar", lambda E: E.activation(g_sb[:, 0:MT], psA[:, 0:MT], AF.Silu), sS)
            wait("vector", tS, umm)
            wait("vector", sS, sg)
            wait("vector", tS, h_rd)
            wait("vector", vS, last_hm)
            hm32 = ins("vector", lambda E: E.tensor_tensor(h32_sb[:, 0:MT], g_sb[:, 0:MT],
                                                           psA[:, 512:512 + MT], ALU.mult), vS)
            last_g_rd = hm32
            wait("vector", vS, hm32)
            hs1 = ins("vector", lambda E, hg=hg, mv=mv: E.tensor_copy(
                hh_sb[:, hg * TS + mv * MT: hg * TS + (mv + 1) * MT], h32_sb[:, 0:MT]), vS)
            wait("vector", vS, hs1)
            hs2 = ins("vector", lambda E, hg=hg, mv=mv: E.tensor_tensor(
                hlo_sb[:, hg * TS + mv * MT: hg * TS + (mv + 1) * MT], h32_sb[:, 0:MT],
                hh_sb[:, hg * TS + mv * MT: hg * TS + (mv + 1) * MT], ALU.subtract), vS)
            last_hm = hs2
            h_ready[(hg, mv)] = hs2

        if hg == HGRP - 1:
            wait("tensor", vS, last_hm)
            for dc in range(NDC):
                for mv in range(NMT):
                    db = (dc * NMT + mv) % 2
                    psDn = psE if db == 0 else psF
                    wait("tensor", vS, last_psd_rd[db])
                    for g in range(HGRP):
                        hcg = hc - (HGRP - 1) + g
                        wslg = hcg % 2
                        mm3(lambda psDn=psDn: psDn[:, 0:MT],
                            lambda wslg=wslg, dc=dc: wdh[wslg][:, dc * 128:(dc + 1) * 128],
                            lambda wslg=wslg, dc=dc: wdl[wslg][:, dc * 128:(dc + 1) * 128],
                            lambda g=g, mv=mv: hh_sb[:, g * TS + mv * MT: g * TS + (mv + 1) * MT],
                            lambda g=g, mv=mv: hlo_sb[:, g * TS + mv * MT: g * TS + (mv + 1) * MT],
                            start=(g == 0), stop=(g == HGRP - 1), ctr=tS if g == HGRP - 1 else None)
                    dm = tS.n
                    last_wd_rd[0] = dm
                    last_wd_rd[1] = dm
                    wait("vector", tS, dm)
                    if hc < HGRP:
                        av = ins("vector", lambda E, dc=dc, mv=mv, psDn=psDn: E.tensor_copy(
                            acc[:, dc * OWN + mv * MT: dc * OWN + mv * MT + MT], psDn[:, 0:MT]), vS)
                    else:
                        wait("vector", vS, acc_written[(dc, mv)])
                        av = ins("vector", lambda E, dc=dc, mv=mv, psDn=psDn: E.tensor_tensor(
                            acc[:, dc * OWN + mv * MT: dc * OWN + mv * MT + MT],
                            acc[:, dc * OWN + mv * MT: dc * OWN + mv * MT + MT],
                            psDn[:, 0:MT], ALU.add), vS)
                    acc_written[(dc, mv)] = av
                    last_psd_rd[db] = av
            h_rd = tS.n
    barrier()
    esF.close()

    es4 = contextlib.ExitStack()
    oh_sb = es4.enter_context(nc.sbuf_tensor("oh_sb", [128, NDC * OWN], BF16))
    ol_sb = es4.enter_context(nc.sbuf_tensor("ol_sb", [128, NDC * OWN], BF16))
    o32_sb = es4.enter_context(nc.sbuf_tensor("o32_sb", [128, 4 * OWN], F32))
    wouth = [es4.enter_context(nc.sbuf_tensor(f"wouth{i}", [128, D], BF16)) for i in range(2)]
    woutl = [es4.enter_context(nc.sbuf_tensor(f"woutl{i}", [128, D], BF16)) for i in range(2)]

    wait("gpsimd", cc, 1)
    o_splits = {}
    for eb in range(4):
        wait("gpsimd", vS, o_splits.get(eb - 1, 0))
        def dynload(E, eb=eb):
            if eb == 0:
                E.reg_mov(dynregs["lin"], dynregs["coff"])
            r = E.dma_start(out=o32_sb[:].rearrange("p (b n) -> p b n", b=4),
                            in_=bass.AP(gout, dynregs["lin"], [[S, 128], [128 * S, 4], [1, OWN]]))
            E.reg_add(dynregs["lin"], dynregs["lin"], 4 * 128 * S)
            return r
        gl = ins("gpsimd", dynload, dR)
        wait("vector", dR, gl)
        s2 = 0
        for i in range(4):
            ec = eb * 4 + i
            s1 = ins("vector", lambda E, ec=ec, i=i: E.tensor_copy(
                oh_sb[:, ec * OWN:(ec + 1) * OWN], o32_sb[:, i * OWN:(i + 1) * OWN]), vS)
            wait("vector", vS, s1)
            s2 = ins("vector", lambda E, ec=ec, i=i: E.tensor_tensor(
                ol_sb[:, ec * OWN:(ec + 1) * OWN], o32_sb[:, i * OWN:(i + 1) * OWN],
                oh_sb[:, ec * OWN:(ec + 1) * OWN], ALU.subtract), vS)
        o_splits[eb] = s2
    o_ready = vS.n

    last_wo_rd = [0, 0]
    last_psP_rd4 = [0, 0]
    for dc in range(NDC):
        wsl = dc % 2
        wait("sync", vS, last_stage_rd[wsl])
        wold = ins("sync", lambda E, dc=dc, wsl=wsl: E.dma_start(
            out=wstage[wsl][:], in_=woutt[dc:dc + 1].rearrange("o p d -> (o p) d")), dW[wsl])
        wait("vector", dW[wsl], wold)
        wait("vector", tS, last_wo_rd[wsl])
        w1 = ins("vector", lambda E, wsl=wsl: E.tensor_copy(wouth[wsl][:], wstage[wsl][:]), vS)
        wait("vector", vS, w1)
        w2 = ins("vector", lambda E, wsl=wsl: E.tensor_tensor(
            woutl[wsl][:], wstage[wsl][:], wouth[wsl][:], ALU.subtract), vS)
        last_stage_rd[wsl] = w2
        for mv in range(NMT):
            pb = (dc * NMT + mv) % 2
            psP = psE if pb == 0 else psF
            wait("tensor", vS, w2)
            wait("tensor", vS, o_ready)
            wait("tensor", vS, last_psP_rd4[pb])
            for ec in range(NDC):
                mm3(lambda psP=psP: psP[:, 0:MT],
                    lambda wsl=wsl, ec=ec: wouth[wsl][:, ec * 128:(ec + 1) * 128],
                    lambda wsl=wsl, ec=ec: woutl[wsl][:, ec * 128:(ec + 1) * 128],
                    lambda ec=ec, mv=mv: oh_sb[:, ec * OWN + mv * MT: ec * OWN + mv * MT + MT],
                    lambda ec=ec, mv=mv: ol_sb[:, ec * OWN + mv * MT: ec * OWN + mv * MT + MT],
                    start=(ec == 0), stop=(ec == NDC - 1), ctr=tS if ec == NDC - 1 else None)
            mmn = tS.n
            last_wo_rd[wsl] = mmn
            wait("vector", tS, mmn)
            wait("vector", vS, acc_written[(dc, mv)])
            av = ins("vector", lambda E, dc=dc, mv=mv, psP=psP: E.tensor_tensor(
                acc[:, dc * OWN + mv * MT: dc * OWN + mv * MT + MT],
                acc[:, dc * OWN + mv * MT: dc * OWN + mv * MT + MT],
                psP[:, 0:MT], ALU.add), vS)
            acc_written[(dc, mv)] = av
            last_psP_rd4[pb] = av
    barrier()
    es4.close()

    xres = [es5.enter_context(nc.sbuf_tensor(f"xres{i}", [128, D], F32)) for i in range(2)]
    outb = [es5.enter_context(nc.sbuf_tensor(f"outb{i}", [128, D], F32)) for i in range(2)]
    last_out_store = [0, 0]
    last_xres_rd = [0, 0]
    last_pst_rd = [0, 0]
    for qs in range(OWN // 128):
        sl = qs % 2
        wait("sync", vS, last_xres_rd[sl])
        xl_ = ins("sync", lambda E, qs=qs, sl=sl: E.dma_start(
            out=xres[sl][:], in_=x[qs * 128:(qs + 1) * 128, :]), dX[sl])
        for dc in range(NDC):
            bk = dc % 2
            pst = psC if bk == 0 else psD
            wait("tensor", vS, last_pst_rd[bk])
            tt = ins("tensor", lambda E, dc=dc, qs=qs, pst=pst: E.transpose(
                pst[:, 0:128], acc[:, dc * OWN + qs * 128: dc * OWN + (qs + 1) * 128], id_sb[:]), tS)
            wait("vector", tS, tt)
            wait("vector", dX[sl], xl_)
            if dc == 0:
                wait("vector", dG[sl], last_out_store[sl])
            ad = ins("vector", lambda E, dc=dc, sl=sl, pst=pst: E.tensor_tensor(
                outb[sl][:, dc * 128:(dc + 1) * 128], pst[:, 0:128],
                xres[sl][:, dc * 128:(dc + 1) * 128], ALU.add), vS)
            last_pst_rd[bk] = ad
            if dc == NDC - 1:
                last_xres_rd[sl] = ad
        wait("gpsimd", vS, vS.n)
        st = ins("gpsimd", lambda E, qs=qs, sl=sl: E.dma_start(
            out=out[qs * 128:(qs + 1) * 128, :], in_=outb[sl][:]), dG[sl])
        last_out_store[sl] = st

    for c in ALL_DMA:
        wait("gpsimd", c)

    with nc.Block() as block:
        @block.sync
        def _(E):
            for f in prog["sync"]:
                f(E)

        @block.tensor
        def _(E):
            for f in prog["tensor"]:
                f(E)

        @block.vector
        def _(E):
            for f in prog["vector"]:
                f(E)

        @block.scalar
        def _(E):
            for f in prog["scalar"]:
                f(E)

        @block.gpsimd
        def _(E):
            for f in prog["gpsimd"]:
                f(E)

    es5.close()
    es.close()
    return nc


def _retile(wT, ng):
    d_in = wT.shape[0]
    ndc = d_in // 128
    return np.ascontiguousarray(
        wT.reshape(ndc, 128, ng, 128).transpose(2, 1, 0, 3).reshape(ng, 128, d_in))


def prepare_in_maps(x2d, Wqkv, Wout, w_gate, w_up, w_down, S):
    wqkvt = _retile(np.ascontiguousarray(Wqkv.T), 48)
    woutt = _retile(np.ascontiguousarray(Wout.T), 16)
    wgt = _retile(np.ascontiguousarray(w_gate.T), 64)
    wut = _retile(np.ascontiguousarray(w_up.T), 64)
    wdt = np.ascontiguousarray(w_down.T)
    ident = np.eye(128, dtype=np.float32)
    in_maps = []
    for c in range(8):
        xc = np.ascontiguousarray(np.roll(x2d, -(S // 8) * c, axis=0))
        slabs = np.ascontiguousarray(wqkvt[[16 + 2 * c, 16 + 2 * c + 1,
                                            32 + 2 * c, 32 + 2 * c + 1,
                                            2 * c, 2 * c + 1]])
        in_maps.append({
            "x": xc, "wqkvt6": slabs, "woutt": woutt,
            "wgt": wgt, "wut": wut, "wdt": wdt, "ident": ident,
            "coff": np.array([[(S // 8) * c]], dtype=np.int32),
        })
    return in_maps


def kernel(L0_attn_x, L0_attn_Wqkv, L0_attn_Wout, L0_ffn_w_gate, L0_ffn_w_up, L0_ffn_w_down):
    S = 8192
    x2d = np.ascontiguousarray(np.asarray(L0_attn_x, dtype=np.float32).reshape(S, D))
    in_maps = prepare_in_maps(
        x2d, np.asarray(L0_attn_Wqkv, dtype=np.float32),
        np.asarray(L0_attn_Wout, dtype=np.float32),
        np.asarray(L0_ffn_w_gate, dtype=np.float32),
        np.asarray(L0_ffn_w_up, dtype=np.float32),
        np.asarray(L0_ffn_w_down, dtype=np.float32), S)
    nc = build(S)
    res = run_bass_kernel_spmd(nc, in_maps, core_ids=list(range(8)))
    full = np.concatenate([res.results[c]["out"] for c in range(8)], axis=0)
    return full.reshape(1, S, D).astype(np.float32)


# version 11 (final)
# speedup vs baseline: 1.0543x; 1.0543x over previous
import contextlib
import numpy as np

import concourse.bass as bass
import concourse.mybir as mybir
from concourse.bass_utils import run_bass_kernel_spmd

F32 = mybir.dt.float32
BF16 = mybir.dt.bfloat16
I32 = mybir.dt.int32
AF = mybir.ActivationFunctionType
ALU = mybir.AluOpType
AX = mybir.AxisListType

D = 2048
H = 16
HL = 2
DH = 128
HID = 8192
NDC = D // 128
NHC = HID // 128
SCALE = 1.0 / np.sqrt(128.0)
SCALE_HI = float(np.float32(SCALE))
SCALE_LO = float(np.float32(SCALE - SCALE_HI))


class Ctr:
    def __init__(self, nc, es, name, step=1):
        self.h = es.enter_context(nc.semaphore(name))
        self.n = 0
        self.step = step


def build(S, mt=None):
    OWN = S // 8
    TS = OWN
    NSUB = TS // 128
    MT = mt if mt is not None else min(512, TS)
    NMT = TS // MT
    NQT = S // MT
    NKC = S // 128
    NPAIR = NKC // 2
    NQC = MT // 128

    nc = bass.Bass("TRN2", target_bir_lowering=False)

    x = nc.dram_tensor("x", [S, D], F32, kind="ExternalInput").ap()
    wqkvt6 = nc.dram_tensor("wqkvt6", [6, 128, D], F32, kind="ExternalInput").ap()
    woutt = nc.dram_tensor("woutt", [16, 128, D], F32, kind="ExternalInput").ap()
    wgt = nc.dram_tensor("wgt", [NHC, 128, D], F32, kind="ExternalInput").ap()
    wut = nc.dram_tensor("wut", [NHC, 128, D], F32, kind="ExternalInput").ap()
    wdt = nc.dram_tensor("wdt", [HID, D], F32, kind="ExternalInput").ap()
    ident = nc.dram_tensor("ident", [128, 128], F32, kind="ExternalInput").ap()
    coff_in = nc.dram_tensor("coff", [1, 1], I32, kind="ExternalInput").ap()
    out = nc.dram_tensor("out", [OWN, D], F32, kind="ExternalOutput").ap()
    kth_h = nc.dram_tensor("kth_h", [HL * 128, S], BF16).ap()
    ktl_h = nc.dram_tensor("ktl_h", [HL * 128, S], BF16).ap()
    vT_h = nc.dram_tensor("vT_h", [HL * 128, S], F32).ap()
    qT_h = nc.dram_tensor("qT_h", [HL * 128, S], F32).ap()
    oT_loc = nc.dram_tensor("oT_loc", [HL * 128, S], F32)
    oT_loc_ap = oT_loc.ap()
    gout = nc.dram_tensor("gout", [8 * HL * 128, S], F32)
    gout_ap = gout.ap()
    xnh_h = nc.dram_tensor("xnh_h", [128, NDC * S // 8 * 8 // 8], BF16)
    del xnh_h
    xnh_hbm = nc.dram_tensor("xnh_hbm", [128, NDC * OWN], BF16).ap()
    xnl_hbm = nc.dram_tensor("xnl_hbm", [128, NDC * OWN], BF16).ap()

    es = contextlib.ExitStack()
    id_sb = es.enter_context(nc.sbuf_tensor("id_sb", [128, 128], F32))
    onescol = es.enter_context(nc.sbuf_tensor("onescol", [128, 1], F32))
    onesrow_bf = es.enter_context(nc.sbuf_tensor("onesrow_bf", [1, 128], BF16))
    onesrow32 = es.enter_context(nc.sbuf_tensor("onesrow32", [1, 128], F32))
    eps_sb = es.enter_context(nc.sbuf_tensor("eps_sb", [128, 1], F32))
    mean_sb = es.enter_context(nc.sbuf_tensor("mean_sb", [128, 1], F32))
    ex2_sb = es.enter_context(nc.sbuf_tensor("ex2_sb", [128, 1], F32))
    msq_sb = es.enter_context(nc.sbuf_tensor("msq_sb", [128, 1], F32))
    var_sb = es.enter_context(nc.sbuf_tensor("var_sb", [128, 1], F32))
    std_sb = es.enter_context(nc.sbuf_tensor("std_sb", [128, 1], F32))
    rstd_sb = es.enter_context(nc.sbuf_tensor("rstd_sb", [128, 1], F32))
    rtmp_sb = es.enter_context(nc.sbuf_tensor("rtmp_sb", [128, 1], F32))
    coff_sb = es.enter_context(nc.sbuf_tensor("coff_sb", [1, 1], I32))
    m_sb = es.enter_context(nc.sbuf_tensor("m_sb", [128, 4], F32))
    mneg_sb = es.enter_context(nc.sbuf_tensor("mneg_sb", [128, 4], F32))
    mrow_sb = es.enter_context(nc.sbuf_tensor("mrow_sb", [1, 512], BF16))
    rmax_sb = es.enter_context(nc.sbuf_tensor("rmax_sb", [128, 1], F32))

    psA = es.enter_context(nc.psum_tensor("psA", [128, 1024], F32))
    psB = es.enter_context(nc.psum_tensor("psB", [128, 1024], F32))
    psC = es.enter_context(nc.psum_tensor("psC", [128, 512], F32))
    psD = es.enter_context(nc.psum_tensor("psD", [128, 512], F32))
    psE = es.enter_context(nc.psum_tensor("psE", [128, 512], F32))
    psF = es.enter_context(nc.psum_tensor("psF", [128, 512], F32))

    tS = Ctr(nc, es, "tS")
    vS = Ctr(nc, es, "vS")
    sS = Ctr(nc, es, "sS")
    gS = Ctr(nc, es, "gS")
    cc = Ctr(nc, es, "cc")
    dX = [Ctr(nc, es, "dX0", 16), Ctr(nc, es, "dX1", 16)]
    dW = [Ctr(nc, es, "dW0", 16), Ctr(nc, es, "dW1", 16)]
    dU = [Ctr(nc, es, "dU0", 16), Ctr(nc, es, "dU1", 16)]
    dDm = [Ctr(nc, es, "dD0", 16), Ctr(nc, es, "dD1", 16)]
    dK = [Ctr(nc, es, "dK0", 16), Ctr(nc, es, "dK1", 16)]
    dV = Ctr(nc, es, "dV", 16)
    dQ = Ctr(nc, es, "dQ", 16)
    dO = Ctr(nc, es, "dO", 16)
    dG = [Ctr(nc, es, "dG0", 16), Ctr(nc, es, "dG1", 16)]
    dR = Ctr(nc, es, "dR", 16)
    ALL_DMA = dX + dW + dU + dDm + dK + [dV, dQ, dO, dR] + dG

    prog = {k: [] for k in ("tensor", "vector", "scalar", "sync", "gpsimd")}
    hw = {}

    def em(e, f):
        prog[e].append(f)

    def wait(e, ctr, n=None):
        v = ctr.n if n is None else n
        if v <= 0:
            return
        key = (e, ctr.h.name)
        if hw.get(key, 0) >= v:
            return
        hw[key] = v
        em(e, lambda E, h=ctr.h, v=v: E.wait_ge(h, v))

    def ins(e, f, ctr=None):
        if ctr is None:
            em(e, lambda E, f=f: f(E))
            return None
        ctr.n += ctr.step
        em(e, lambda E, f=f, h=ctr.h, s=ctr.step: f(E).then_inc(h, s))
        return ctr.n

    def barrier():
        cts = [tS, vS, sS, gS] + ALL_DMA
        for e in prog:
            for c in cts:
                wait(e, c)

    def mm3(ps_ap_f, lh, ll, rh, rl, start, stop, ctr):
        ins("tensor", lambda E, a=ps_ap_f, b=lh, c=rh, st=start: E.matmul(
            a(), b(), c(), start=st, stop=False), None)
        ins("tensor", lambda E, a=ps_ap_f, b=lh, c=rl: E.matmul(
            a(), b(), c(), start=False, stop=False), None)
        return ins("tensor", lambda E, a=ps_ap_f, b=ll, c=rh, sp=stop: E.matmul(
            a(), b(), c(), start=False, stop=sp), ctr)

    ins("sync", lambda E: E.dma_start(out=id_sb[:], in_=ident), dO)
    ins("sync", lambda E: E.dma_start(out=coff_sb[:], in_=coff_in), dO)
    dynregs = {}

    def dyninit(E):
        dynregs["coff"] = E.alloc_register("r_coff")
        dynregs["lin"] = E.alloc_register("r_lin")
        dynregs["row"] = E.alloc_register("r_row")
        dynregs["off"] = E.alloc_register("r_off")
        E.reg_load(dynregs["coff"], coff_sb[:1, :1])
    wait("gpsimd", dO)
    ins("gpsimd", dyninit)
    ins("gpsimd", lambda E: E.memset(onescol[:], 1.0), gS)
    ins("gpsimd", lambda E: E.memset(onesrow_bf[:], 1.0), gS)
    ins("gpsimd", lambda E: E.memset(onesrow32[:], 1.0), gS)
    ins("gpsimd", lambda E: E.memset(eps_sb[:], 1e-5), gS)
    barrier()

    es12 = contextlib.ExitStack()
    xnh = es12.enter_context(nc.sbuf_tensor("xnh", [128, NDC * TS], BF16))
    xnl = es12.enter_context(nc.sbuf_tensor("xnl", [128, NDC * TS], BF16))
    xrow = [es12.enter_context(nc.sbuf_tensor(f"xrow{i}", [128, D], F32)) for i in range(2)]
    sq_sb = es12.enter_context(nc.sbuf_tensor("sq_sb", [128, D], F32))
    wf_sb = [es12.enter_context(nc.sbuf_tensor(f"wf{i}", [128, D], F32)) for i in range(2)]
    wh_sb = [es12.enter_context(nc.sbuf_tensor(f"wh{i}", [128, D], BF16)) for i in range(2)]
    wl_sb = [es12.enter_context(nc.sbuf_tensor(f"wl{i}", [128, D], BF16)) for i in range(2)]
    evh_sb = [es12.enter_context(nc.sbuf_tensor(f"evh{i}", [128, 512], BF16)) for i in range(2)]
    evl_sb = [es12.enter_context(nc.sbuf_tensor(f"evl{i}", [128, 512], BF16)) for i in range(2)]
    ev32_sb = [es12.enter_context(nc.sbuf_tensor(f"ev32{i}", [128, 512], F32)) for i in range(2)]

    last_xrow_reader = [0, 0]
    last_evac_store = [0, 0]
    last_wslab_reader = [0, 0]
    last_psT_rd_v = [0, 0]
    last_psT_rd_s = [0, 0]
    last_psP_rd = [0, 0]
    last_proj_mm_prev = 0
    last_psP_rd_s2 = [0, 0]
    last_split_wv = 0
    xq_evac = {}

    supertiles = list(range(1, 8)) + [0]
    for t_i, t in enumerate(supertiles):
        for j in range(NSUB):
            sl = j % 2
            wait("sync", tS, last_xrow_reader[sl])
            ld = ins("sync", lambda E, t=t, j=j, sl=sl: E.dma_start(
                out=xrow[sl][:], in_=x[t * TS + 128 * j: t * TS + 128 * (j + 1), :]), dX[sl])
            wait("vector", dX[sl], ld)
            n1 = ins("vector", lambda E, sl=sl: E.reduce_sum(mean_sb[:], xrow[sl][:], axis=AX.X), vS)
            wait("vector", vS, n1)
            n2 = ins("vector", lambda E: E.tensor_scalar_mul(mean_sb[:], mean_sb[:], 1.0 / D), vS)
            wait("scalar", dX[sl], ld)
            m1 = ins("scalar", lambda E, sl=sl: E.activation(sq_sb[:], xrow[sl][:], AF.Square), sS)
            wait("vector", sS, m1)
            n3 = ins("vector", lambda E: E.reduce_sum(ex2_sb[:], sq_sb[:], axis=AX.X), vS)
            wait("vector", vS, n3)
            n4 = ins("vector", lambda E: E.tensor_scalar_mul(ex2_sb[:], ex2_sb[:], 1.0 / D), vS)
            wait("vector", vS, n4)
            n5 = ins("vector", lambda E: E.tensor_tensor(msq_sb[:], mean_sb[:], mean_sb[:], ALU.mult), vS)
            wait("vector", vS, n5)
            n6 = ins("vector", lambda E: E.tensor_tensor(var_sb[:], ex2_sb[:], msq_sb[:], ALU.subtract), vS)
            wait("scalar", vS, n6)
            m2 = ins("scalar", lambda E: E.activation(std_sb[:], var_sb[:], AF.Sqrt, bias=eps_sb[:]), sS)
            wait("vector", sS, m2)
            n7 = ins("vector", lambda E: E.reciprocal(rstd_sb[:], std_sb[:]), vS)
            wait("vector", vS, n7)
            n7b = ins("vector", lambda E: E.tensor_tensor(rtmp_sb[:], std_sb[:], rstd_sb[:], ALU.mult), vS)
            wait("vector", vS, n7b)
            n7c = ins("vector", lambda E: E.tensor_scalar(rtmp_sb[:], rtmp_sb[:], -1.0, 2.0, ALU.mult, ALU.add), vS)
            wait("vector", vS, n7c)
            n7d = ins("vector", lambda E: E.tensor_tensor(rstd_sb[:], rstd_sb[:], rtmp_sb[:], ALU.mult), vS)
            wait("vector", vS, n7d)
            n8 = ins("vector", lambda E, sl=sl: E.tensor_scalar(
                xrow[sl][:], xrow[sl][:], mean_sb[:], rstd_sb[:], ALU.subtract, ALU.mult), vS)
            wait("tensor", vS, n8)
            for ci in range(NDC):
                bk = ci % 2
                pst = psC if bk == 0 else psD
                wait("tensor", vS, last_psT_rd_v[bk])
                wait("tensor", sS, last_psT_rd_s[bk])
                t1 = ins("tensor", lambda E, sl=sl, ci=ci, pst=pst: E.transpose(
                    pst[:, 0:128], xrow[sl][:, ci * 128:(ci + 1) * 128], id_sb[:]), tS)
                wait("scalar", tS, t1)
                if j == 0 and ci == 0 and t_i > 0:
                    wait("scalar", tS, last_proj_mm_prev)
                    wait("vector", tS, last_proj_mm_prev)
                e1 = ins("scalar", lambda E, pst=pst, ci=ci, j=j: E.activation(
                    xnh[:, ci * TS + 128 * j: ci * TS + 128 * (j + 1)], pst[:, 0:128], AF.Copy), sS)
                last_psT_rd_s[bk] = e1
                wait("vector", sS, e1)
                e2 = ins("vector", lambda E, pst=pst, ci=ci, j=j: E.scalar_tensor_tensor(
                    xnl[:, ci * TS + 128 * j: ci * TS + 128 * (j + 1)], pst[:, 0:128], 1.0,
                    xnh[:, ci * TS + 128 * j: ci * TS + 128 * (j + 1)], ALU.mult, ALU.subtract), vS)
                last_psT_rd_v[bk] = e2
                if ci == NDC - 1:
                    last_xrow_reader[sl] = t1
        a_ready = vS.n

        dests = [(kth_h, ktl_h, 0), (kth_h, ktl_h, 128), (vT_h, None, 0), (vT_h, None, 128),
                 (qT_h, None, 0), (qT_h, None, 128)]
        for g in range(6):
            wsl = g % 2
            dh, dl, rbase = dests[g]
            wait("sync", tS, last_wslab_reader[wsl])
            wait("sync", vS, last_split_wv)
            wl_ld = ins("sync", lambda E, g=g, wsl=wsl: E.dma_start(
                out=wf_sb[wsl][:], in_=wqkvt6[g:g + 1].rearrange("o p d -> (o p) d")), dW[wsl])
            wait("vector", dW[wsl], wl_ld)
            wait("vector", tS, last_wslab_reader[wsl])
            s1 = ins("vector", lambda E, wsl=wsl: E.tensor_copy(wh_sb[wsl][:], wf_sb[wsl][:]), vS)
            wait("vector", vS, s1)
            s2 = ins("vector", lambda E, wsl=wsl: E.tensor_tensor(
                wl_sb[wsl][:], wf_sb[wsl][:], wh_sb[wsl][:], ALU.subtract), vS)
            last_split_wv = s2
            for mv in range(NMT):
                pb = (g * NMT + mv) % 2
                psP = psE if pb == 0 else psF
                wait("tensor", vS, s2)
                if g == 0 and mv == 0:
                    wait("tensor", vS, a_ready)
                    wait("tensor", sS, sS.n)
                wait("tensor", vS, last_psP_rd[pb])
                wait("tensor", sS, last_psP_rd_s2[pb])
                for ci in range(NDC):
                    mm3(lambda ci=ci, psP=psP: psP[:, 0:MT],
                        lambda wsl=wsl, ci=ci: wh_sb[wsl][:, ci * 128:(ci + 1) * 128],
                        lambda wsl=wsl, ci=ci: wl_sb[wsl][:, ci * 128:(ci + 1) * 128],
                        lambda ci=ci, mv=mv: xnh[:, ci * TS + mv * MT: ci * TS + mv * MT + MT],
                        lambda ci=ci, mv=mv: xnl[:, ci * TS + mv * MT: ci * TS + mv * MT + MT],
                        start=(ci == 0), stop=(ci == NDC - 1),
                        ctr=tS if ci == NDC - 1 else None)
                mmn = tS.n
                last_wslab_reader[wsl] = mmn
                evsl = pb
                if dl is not None:
                    wait("scalar", tS, mmn)
                    wait("scalar", dG[evsl], last_evac_store[evsl])
                    eh = ins("scalar", lambda E, psP=psP, evsl=evsl: E.activation(
                        evh_sb[evsl][:, 0:MT], psP[:, 0:MT], AF.Copy), sS)
                    wait("vector", sS, eh)
                    wait("vector", dG[evsl], last_evac_store[evsl])
                    el = ins("vector", lambda E, psP=psP, evsl=evsl: E.scalar_tensor_tensor(
                        evl_sb[evsl][:, 0:MT], psP[:, 0:MT], 1.0, evh_sb[evsl][:, 0:MT],
                        ALU.mult, ALU.subtract), vS)
                    last_psP_rd[pb] = el
                    last_psP_rd_s2[pb] = eh
                    wait("gpsimd", vS, el)
                    st1 = ins("gpsimd", lambda E, dh=dh, rbase=rbase, t=t, mv=mv, evsl=evsl: E.dma_start(
                        out=dh[rbase:rbase + 128, t * TS + mv * MT: t * TS + mv * MT + MT],
                        in_=evh_sb[evsl][:, 0:MT]), dG[evsl])
                    st2 = ins("gpsimd", lambda E, dl=dl, rbase=rbase, t=t, mv=mv, evsl=evsl: E.dma_start(
                        out=dl[rbase:rbase + 128, t * TS + mv * MT: t * TS + mv * MT + MT],
                        in_=evl_sb[evsl][:, 0:MT]), dG[evsl])
                    last_evac_store[evsl] = st2
                else:
                    wait("vector", tS, mmn)
                    wait("vector", dG[evsl], last_evac_store[evsl])
                    ev = ins("vector", lambda E, psP=psP, evsl=evsl: E.tensor_copy(
                        ev32_sb[evsl][:, 0:MT], psP[:, 0:MT]), vS)
                    last_psP_rd[pb] = ev
                    wait("gpsimd", vS, ev)
                    st1 = ins("gpsimd", lambda E, dh=dh, rbase=rbase, t=t, mv=mv, evsl=evsl: E.dma_start(
                        out=dh[rbase:rbase + 128, t * TS + mv * MT: t * TS + mv * MT + MT],
                        in_=ev32_sb[evsl][:, 0:MT]), dG[evsl])
                    last_evac_store[evsl] = st1
        last_proj_mm_prev = tS.n

    wait("gpsimd", vS)
    wait("gpsimd", tS)
    wait("gpsimd", dG[0])
    wait("gpsimd", dG[1])
    ins("gpsimd", lambda E: E.dma_start(out=xnh_hbm, in_=xnh[:, :]), dG[0])
    ins("gpsimd", lambda E: E.dma_start(out=xnl_hbm, in_=xnl[:, :]), dG[1])
    barrier()
    es12.close()

    es3 = contextlib.ExitStack()
    kth_sb = es3.enter_context(nc.sbuf_tensor("kth_sb", [128, 2 * S], BF16))
    ktl_sb = es3.enter_context(nc.sbuf_tensor("ktl_sb", [128, 2 * S], BF16))
    v32_sb = es3.enter_context(nc.sbuf_tensor("v32_sb", [128, S], F32))
    vh_sb = es3.enter_context(nc.sbuf_tensor("vh_sb", [128, S], BF16))
    vl_sb = es3.enter_context(nc.sbuf_tensor("vl_sb", [128, S], BF16))
    qsh_sb = es3.enter_context(nc.sbuf_tensor("qsh_sb", [128, S], BF16))
    qsl_sb = es3.enter_context(nc.sbuf_tensor("qsl_sb", [128, S], BF16))
    q32_sb = es3.enter_context(nc.sbuf_tensor("q32_sb", [128, 1024], F32))
    qt32_sb = es3.enter_context(nc.sbuf_tensor("qt32_sb", [128, 1024], F32))
    p32_sb = [es3.enter_context(nc.sbuf_tensor(f"p32_{i}", [128, 1024], F32)) for i in range(2)]
    ph_sb = [es3.enter_context(nc.sbuf_tensor(f"ph_{i}", [128, 1024], BF16)) for i in range(2)]
    pl_sb = [es3.enter_context(nc.sbuf_tensor(f"pl_{i}", [128, 1024], BF16)) for i in range(2)]
    z_sb = es3.enter_context(nc.sbuf_tensor("z_sb", [128, 1024], F32))
    zb_sb = es3.enter_context(nc.sbuf_tensor("zb_sb", [128, 512], F32))
    zr_sb = es3.enter_context(nc.sbuf_tensor("zr_sb", [1, 512], F32))
    zt_sb = es3.enter_context(nc.sbuf_tensor("zt_sb", [1, 512], F32))
    ot_sb = es3.enter_context(nc.sbuf_tensor("ot_sb", [128, OWN], F32))

    prev_head_tS = 0
    prev_head_vS = 0
    prev_head_sS = 0
    prev_ot_store = 0
    exps = []
    zs = []
    avs = []
    p1_reads = []
    psef_rd = [0, 0]
    p1_i = 0
    ot_read = 0
    last_q32_rd = 0
    gsp2s = []
    gzs = []
    last_zfin = 0

    for hl in range(HL):
        half = hl % 2
        wait("sync", tS, prev_head_tS)
        klh = ins("sync", lambda E, hl=hl, half=half: E.dma_start(
            out=kth_sb[:, half * S:(half + 1) * S], in_=kth_h[hl * 128:(hl + 1) * 128, :]), dK[half])
        kll = ins("sync", lambda E, hl=hl, half=half: E.dma_start(
            out=ktl_sb[:, half * S:(half + 1) * S], in_=ktl_h[hl * 128:(hl + 1) * 128, :]), dK[half])
        wait("sync", vS, prev_head_vS)
        wait("sync", sS, prev_head_sS)
        wait("sync", dG[0], prev_ot_store)
        vl32 = ins("sync", lambda E, hl=hl: E.dma_start(
            out=v32_sb[:], in_=vT_h[hl * 128:(hl + 1) * 128, :]), dV)

        wait("tensor", dV, vl32)
        wait("tensor", vS, prev_head_vS)
        wait("tensor", sS, prev_head_sS)
        for kc in range(NKC):
            bk = kc % 2
            pst = psC if bk == 0 else psD
            wait("tensor", vS, last_psT_rd_v[bk])
            wait("tensor", sS, last_psT_rd_s[bk])
            t1 = ins("tensor", lambda E, kc=kc, pst=pst: E.transpose(
                pst[:, 0:128], v32_sb[:, kc * 128:(kc + 1) * 128], id_sb[:]), tS)
            wait("scalar", tS, t1)
            e1 = ins("scalar", lambda E, kc=kc, pst=pst: E.activation(
                vh_sb[:, kc * 128:(kc + 1) * 128], pst[:, 0:128], AF.Copy), sS)
            last_psT_rd_s[bk] = e1
            wait("vector", sS, e1)
            e2 = ins("vector", lambda E, kc=kc, pst=pst: E.scalar_tensor_tensor(
                vl_sb[:, kc * 128:(kc + 1) * 128], pst[:, 0:128], 1.0,
                vh_sb[:, kc * 128:(kc + 1) * 128], ALU.mult, ALU.subtract), vS)
            last_psT_rd_v[bk] = e2
        v_done = vS.n

        wait("vector", tS, prev_head_tS)
        for qc8 in range(S // 1024):
            wait("sync", vS, last_q32_rd)
            ql = ins("sync", lambda E, hl=hl, qc8=qc8: E.dma_start(
                out=q32_sb[:], in_=qT_h[hl * 128:(hl + 1) * 128, qc8 * 1024:(qc8 + 1) * 1024]), dQ)
            wait("vector", dQ, ql)
            c1 = ins("vector", lambda E: E.tensor_scalar_mul(qt32_sb[:], q32_sb[:], SCALE_LO), vS)
            wait("vector", vS, c1)
            c2 = ins("vector", lambda E: E.tensor_scalar_mul(q32_sb[:], q32_sb[:], SCALE_HI), vS)
            wait("vector", vS, c2)
            c3 = ins("vector", lambda E: E.tensor_tensor(q32_sb[:], q32_sb[:], qt32_sb[:], ALU.add), vS)
            wait("vector", vS, c3)
            c4 = ins("vector", lambda E, qc8=qc8: E.tensor_copy(
                qsh_sb[:, qc8 * 1024:(qc8 + 1) * 1024], q32_sb[:]), vS)
            wait("vector", vS, c4)
            c5 = ins("vector", lambda E, qc8=qc8: E.tensor_tensor(
                qsl_sb[:, qc8 * 1024:(qc8 + 1) * 1024], q32_sb[:],
                qsh_sb[:, qc8 * 1024:(qc8 + 1) * 1024], ALU.subtract), vS)
            last_q32_rd = c5
        q_ready = vS.n

        mrow_done_d = {}
        for qt in range(NQT):
            for qc in range(NQC):
                qg = qt * NQC + qc
                for kt in range(S // 512):
                    pb = p1_i % 2
                    p1_i += 1
                    psP1 = psE if pb == 0 else psF
                    wait("tensor", dK[half], kll)
                    wait("tensor", vS, q_ready)
                    wait("tensor", vS, psef_rd[pb])
                    pm = ins("tensor", lambda E, qg=qg, kt=kt, half=half, psP1=psP1: E.matmul(
                        psP1[:, 0:512],
                        qsh_sb[:, qg * 128:(qg + 1) * 128],
                        kth_sb[:, half * S + kt * 512: half * S + (kt + 1) * 512],
                        start=True, stop=True), tS)
                    wait("vector", tS, pm)
                    if p1_reads:
                        wait("vector", vS, p1_reads[-1])
                    r1 = ins("vector", lambda E, psP1=psP1: E.reduce_max(
                        rmax_sb[:, 0:1], psP1[:, 0:512], axis=AX.X), vS)
                    psef_rd[pb] = r1
                    wait("vector", vS, r1)
                    if kt == 0:
                        r2 = ins("vector", lambda E, qc=qc: E.tensor_copy(m_sb[:, qc:qc + 1], rmax_sb[:, 0:1]), vS)
                    else:
                        r2 = ins("vector", lambda E, qc=qc: E.tensor_tensor(
                            m_sb[:, qc:qc + 1], m_sb[:, qc:qc + 1], rmax_sb[:, 0:1], ALU.max), vS)
                    p1_reads.append(r2)
                wait("vector", vS, p1_reads[-1])
                ng = ins("vector", lambda E, qc=qc: E.tensor_scalar_mul(
                    mneg_sb[:, qc:qc + 1], m_sb[:, qc:qc + 1], -1.0), vS)
                wait("tensor", vS, ng)
                tm = ins("tensor", lambda E, qc=qc: E.matmul(
                    psD[0:1, 0:128], mneg_sb[:, qc:qc + 1], id_sb[:], start=True, stop=True), tS)
                wait("vector", tS, tm)
                mc = ins("vector", lambda E, qc=qc: E.tensor_copy(
                    mrow_sb[0:1, qc * 128:(qc + 1) * 128], psD[0:1, 0:128]), vS)
                p1_reads.append(mc)
            mrow_done_d[qt] = vS.n

            smms = []
            _ = None

            def emit_S(pr, qt=qt, half=half):
                kc0, kc1 = 2 * pr, 2 * pr + 1
                fp = len(smms)
                sb = fp % 2
                psS = psA if sb == 0 else psB
                if fp >= 2:
                    wait("tensor", sS, exps[fp - 2])
                if pr == 0:
                    wait("tensor", vS, mrow_done_d[qt])
                for half_i, kc in enumerate((kc0, kc1)):
                    mm3(lambda psS=psS, half_i=half_i: psS[:, half_i * MT:(half_i + 1) * MT],
                        lambda kc=kc: kth_sb[:, half * S + kc * 128: half * S + (kc + 1) * 128],
                        lambda kc=kc: ktl_sb[:, half * S + kc * 128: half * S + (kc + 1) * 128],
                        lambda: qsh_sb[:, qt * MT:(qt + 1) * MT],
                        lambda: qsl_sb[:, qt * MT:(qt + 1) * MT],
                        start=True, stop=False, ctr=None)
                    last = (half_i == 1)
                    ins("tensor", lambda E, psS=psS, half_i=half_i: E.matmul(
                        psS[:, half_i * MT:(half_i + 1) * MT], onesrow_bf[:],
                        mrow_sb[0:1, 0:MT],
                        start=False, stop=True), tS if last else None)
                smms.append(tS.n)

            emit_S(0)
            for pr in range(NPAIR):
                kc0, kc1 = 2 * pr, 2 * pr + 1
                fp = len(exps)
                sb = fp % 2
                psS = psA if sb == 0 else psB
                if pr + 1 < NPAIR:
                    emit_S(pr + 1)
                wait("scalar", tS, smms[pr])
                if fp >= 2:
                    wait("scalar", vS, zs[fp - 2])
                    wait("scalar", tS, avs[fp - 2])
                ec = ins("scalar", lambda E, sb=sb, psS=psS, MT=MT: E.activation(
                    p32_sb[sb][:, 0:2 * MT], psS[:, 0:2 * MT], AF.Exp), sS)
                exps.append(ec)
                wait("vector", sS, ec)
                if fp >= 2:
                    wait("vector", tS, avs[fp - 2])
                sp1 = ins("vector", lambda E, sb=sb, MT=MT: E.tensor_copy(
                    ph_sb[sb][:, 0:2 * MT], p32_sb[sb][:, 0:2 * MT]), vS)
                wait("vector", vS, sp1)
                sp2 = ins("vector", lambda E, sb=sb, MT=MT: E.tensor_tensor(
                    pl_sb[sb][:, 0:2 * MT], p32_sb[sb][:, 0:2 * MT], ph_sb[sb][:, 0:2 * MT],
                    ALU.subtract), vS)
                gsp2s.append(sp2)
                if pr == 0:
                    wait("vector", tS, last_zfin)
                    zc0 = ins("vector", lambda E, sb=sb, MT=MT: E.tensor_copy(
                        z_sb[:, 0:2 * MT], p32_sb[sb][:, 0:2 * MT]), vS)
                else:
                    wait("vector", vS, zs[-1])
                    zc0 = ins("vector", lambda E, sb=sb, MT=MT: E.tensor_tensor(
                        z_sb[:, 0:2 * MT], z_sb[:, 0:2 * MT], p32_sb[sb][:, 0:2 * MT], ALU.add), vS)
                zs.append(zc0)
                gzs.append(zc0)
                wait("tensor", vS, sp1)
                if pr == 0:
                    wait("tensor", vS, ot_read)
                for half_i, kc in enumerate((kc0, kc1)):
                    ins("tensor", lambda E, kc=kc, sb=sb, half_i=half_i, pr=pr: E.matmul(
                        psC[:, 0:MT], vh_sb[:, kc * 128:(kc + 1) * 128],
                        ph_sb[sb][:, half_i * MT:(half_i + 1) * MT],
                        start=(pr == 0 and half_i == 0), stop=False), None)
                    ins("tensor", lambda E, kc=kc, sb=sb, half_i=half_i: E.matmul(
                        psC[:, 0:MT], vl_sb[:, kc * 128:(kc + 1) * 128],
                        ph_sb[sb][:, half_i * MT:(half_i + 1) * MT],
                        start=False, stop=False), None)
                wait("tensor", vS, sp2)
                for half_i, kc in enumerate((kc0, kc1)):
                    last = (half_i == 1)
                    ins("tensor", lambda E, kc=kc, sb=sb, half_i=half_i, pr=pr: E.matmul(
                        psC[:, 0:MT], vh_sb[:, kc * 128:(kc + 1) * 128],
                        pl_sb[sb][:, half_i * MT:(half_i + 1) * MT],
                        start=False, stop=(pr == NPAIR - 1 and half_i == 1)),
                        tS if last else None)
                avs.append(tS.n)
            av_last = avs[-1]
            wait("tensor", vS, zs[-1])
            ins("tensor", lambda E: E.matmul(psD[0:1, 0:MT], onescol[:], z_sb[:, 0:MT],
                                             start=True, stop=False), None)
            zf = ins("tensor", lambda E: E.matmul(psD[0:1, 0:MT], onescol[:], z_sb[:, MT:2 * MT],
                                                  start=False, stop=True), tS)
            last_zfin = zf
            wait("vector", tS, zf)
            zr1 = ins("vector", lambda E: E.reciprocal(zr_sb[0:1, 0:MT], psD[0:1, 0:MT]), vS)
            wait("vector", vS, zr1)
            zr2 = ins("vector", lambda E: E.tensor_tensor(zt_sb[0:1, 0:MT], psD[0:1, 0:MT], zr_sb[0:1, 0:MT], ALU.mult), vS)
            wait("vector", vS, zr2)
            zr3 = ins("vector", lambda E: E.tensor_scalar(zt_sb[0:1, 0:MT], zt_sb[0:1, 0:MT], -1.0, 2.0, ALU.mult, ALU.add), vS)
            wait("vector", vS, zr3)
            zr4 = ins("vector", lambda E: E.tensor_tensor(zr_sb[0:1, 0:MT], zr_sb[0:1, 0:MT], zt_sb[0:1, 0:MT], ALU.mult), vS)
            wait("tensor", vS, zr4)
            wait("tensor", vS, psef_rd[0])
            zb1 = ins("tensor", lambda E: E.matmul(psE[:, 0:MT], onesrow32[:], zr_sb[0:1, 0:MT], start=True, stop=True), tS)
            wait("vector", tS, zb1)
            wait("vector", vS, ot_read)
            zb2 = ins("vector", lambda E: E.tensor_copy(zb_sb[:, 0:MT], psE[:, 0:MT]), vS)
            psef_rd[0] = zb2
            wait("vector", tS, av_last)
            wait("vector", vS, zb2)
            obase = (qt % (OWN // MT)) * MT
            if obase == 0:
                wait("vector", dG[0], prev_ot_store)
            oc_ = ins("vector", lambda E, obase=obase: E.tensor_tensor(
                ot_sb[:, obase:obase + MT], psC[:, 0:MT], zb_sb[:, 0:MT], ALU.mult), vS)
            ot_read = oc_
            if (qt + 1) % (OWN // MT) == 0:
                wait("gpsimd", vS, oc_)
                def dynstore(E, hl=hl, qt=qt):
                    if qt + 1 == OWN // MT:
                        if hl == 0:
                            E.reg_mov(dynregs["row"], 0)
                        else:
                            E.reg_add(dynregs["row"], dynregs["row"], 128 * S)
                        E.reg_mov(dynregs["lin"], dynregs["coff"])
                    E.reg_mod(dynregs["off"], dynregs["lin"], S)
                    E.reg_add(dynregs["off"], dynregs["off"], dynregs["row"])
                    r = E.dma_start(out=bass.AP(oT_loc, dynregs["off"], [[S, 128], [1, OWN]]),
                                    in_=ot_sb[:, 0:OWN])
                    E.reg_add(dynregs["lin"], dynregs["lin"], OWN)
                    return r
                st = ins("gpsimd", dynstore, dG[0])
                prev_ot_store = st
        prev_head_tS = tS.n
        prev_head_vS = vS.n
        prev_head_sS = sS.n

    wait("gpsimd", dG[0])
    ins("gpsimd", lambda E: E.collective_compute(
        "AllGather", ALU.bypass, replica_groups=[[0, 1, 2, 3, 4, 5, 6, 7]],
        ins=[oT_loc_ap], outs=[gout_ap]), cc)
    barrier()
    es3.close()

    es5 = contextlib.ExitStack()
    acc = es5.enter_context(nc.sbuf_tensor("acc", [128, NDC * OWN], F32))
    wstage = [es5.enter_context(nc.sbuf_tensor(f"wst{i}", [128, D], F32)) for i in range(2)]
    esF = contextlib.ExitStack()
    xnh2 = esF.enter_context(nc.sbuf_tensor("xnh2", [128, NDC * OWN], BF16))
    xnl2 = esF.enter_context(nc.sbuf_tensor("xnl2", [128, NDC * OWN], BF16))
    wgh = [esF.enter_context(nc.sbuf_tensor(f"wgh{i}", [128, D], BF16)) for i in range(2)]
    wgl = [esF.enter_context(nc.sbuf_tensor(f"wgl{i}", [128, D], BF16)) for i in range(2)]
    wuh = [esF.enter_context(nc.sbuf_tensor(f"wuh{i}", [128, D], BF16)) for i in range(2)]
    wul = [esF.enter_context(nc.sbuf_tensor(f"wul{i}", [128, D], BF16)) for i in range(2)]
    wdh = [esF.enter_context(nc.sbuf_tensor(f"wdh{i}", [128, D], BF16)) for i in range(2)]
    wdl = [esF.enter_context(nc.sbuf_tensor(f"wdl{i}", [128, D], BF16)) for i in range(2)]
    h32_sb = esF.enter_context(nc.sbuf_tensor("h32_sb", [128, 512], F32))
    HGRP = 2
    hh_sb = esF.enter_context(nc.sbuf_tensor("hh_sb", [128, HGRP * TS], BF16))
    hlo_sb = esF.enter_context(nc.sbuf_tensor("hlo_sb", [128, HGRP * TS], BF16))
    g_sb = esF.enter_context(nc.sbuf_tensor("g_sb", [128, 512], F32))

    ins("sync", lambda E: E.dma_start(out=xnh2[:, :], in_=xnh_hbm), dO)
    xn2_ld = ins("sync", lambda E: E.dma_start(out=xnl2[:, :], in_=xnl_hbm), dO)

    last_wg_rd = [0, 0]
    last_wu_rd = [0, 0]
    last_wd_rd = [0, 0]
    last_psd_rd = [0, 0]
    last_stage_rd = [0, 0]
    acc_written = {}
    last_hm = 0
    last_h_rd = {}
    last_g_rd = 0

    h_ready = {}
    h_rd = 0
    for hc in range(NHC):
        wsl = hc % 2
        hg = hc % HGRP
        stg = wsl
        wait("sync", vS, last_stage_rd[stg])
        wgld = ins("sync", lambda E, hc=hc, stg=stg: E.dma_start(
            out=wstage[stg][:], in_=wgt[hc:hc + 1].rearrange("o p d -> (o p) d")), dW[stg])
        wait("vector", dW[stg], wgld)
        wait("vector", tS, last_wg_rd[wsl])
        g1 = ins("vector", lambda E, stg=stg, wsl=wsl: E.tensor_copy(wgh[wsl][:], wstage[stg][:]), vS)
        wait("vector", vS, g1)
        g2 = ins("vector", lambda E, stg=stg, wsl=wsl: E.tensor_tensor(
            wgl[wsl][:], wstage[stg][:], wgh[wsl][:], ALU.subtract), vS)
        last_stage_rd[stg] = g2
        wait("sync", vS, g2)
        wuld = ins("sync", lambda E, hc=hc, stg=stg: E.dma_start(
            out=wstage[stg][:], in_=wut[hc:hc + 1].rearrange("o p d -> (o p) d")), dU[stg])
        wait("vector", dU[stg], wuld)
        wait("vector", tS, last_wu_rd[wsl])
        u1 = ins("vector", lambda E, stg=stg, wsl=wsl: E.tensor_copy(wuh[wsl][:], wstage[stg][:]), vS)
        wait("vector", vS, u1)
        u2 = ins("vector", lambda E, stg=stg, wsl=wsl: E.tensor_tensor(
            wul[wsl][:], wstage[stg][:], wuh[wsl][:], ALU.subtract), vS)
        last_stage_rd[stg] = u2
        wait("sync", vS, u2)
        wdld = ins("sync", lambda E, hc=hc, stg=stg: E.dma_start(
            out=wstage[stg][:], in_=wdt[hc * 128:(hc + 1) * 128, :]), dDm[stg])
        wait("vector", dDm[stg], wdld)
        wait("vector", tS, last_wd_rd[wsl])
        d1 = ins("vector", lambda E, stg=stg, wsl=wsl: E.tensor_copy(wdh[wsl][:], wstage[stg][:]), vS)
        wait("vector", vS, d1)
        d2 = ins("vector", lambda E, stg=stg, wsl=wsl: E.tensor_tensor(
            wdl[wsl][:], wstage[stg][:], wdh[wsl][:], ALU.subtract), vS)
        last_stage_rd[stg] = d2

        for mv in range(NMT):
            wait("tensor", vS, u2)
            if hc == 0 and mv == 0:
                wait("tensor", dO, xn2_ld)
            wait("tensor", sS, sS.n)
            wait("tensor", vS, last_hm)
            for ci in range(NDC):
                mm3(lambda: psA[:, 0:MT],
                    lambda wsl=wsl, ci=ci: wgh[wsl][:, ci * 128:(ci + 1) * 128],
                    lambda wsl=wsl, ci=ci: wgl[wsl][:, ci * 128:(ci + 1) * 128],
                    lambda ci=ci, mv=mv: xnh2[:, ci * TS + mv * MT: ci * TS + mv * MT + MT],
                    lambda ci=ci, mv=mv: xnl2[:, ci * TS + mv * MT: ci * TS + mv * MT + MT],
                    start=(ci == 0), stop=(ci == NDC - 1), ctr=tS if ci == NDC - 1 else None)
            gmm = tS.n
            last_wg_rd[wsl] = gmm
            for ci in range(NDC):
                mm3(lambda: psA[:, 512:512 + MT],
                    lambda wsl=wsl, ci=ci: wuh[wsl][:, ci * 128:(ci + 1) * 128],
                    lambda wsl=wsl, ci=ci: wul[wsl][:, ci * 128:(ci + 1) * 128],
                    lambda ci=ci, mv=mv: xnh2[:, ci * TS + mv * MT: ci * TS + mv * MT + MT],
                    lambda ci=ci, mv=mv: xnl2[:, ci * TS + mv * MT: ci * TS + mv * MT + MT],
                    start=(ci == 0), stop=(ci == NDC - 1), ctr=tS if ci == NDC - 1 else None)
            umm = tS.n
            last_wu_rd[wsl] = umm
            wait("scalar", tS, gmm)
            wait("scalar", vS, last_g_rd)
            sg = ins("scalar", lambda E: E.activation(g_sb[:, 0:MT], psA[:, 0:MT], AF.Silu), sS)
            wait("vector", tS, umm)
            wait("vector", sS, sg)
            wait("vector", tS, h_rd)
            wait("vector", vS, last_hm)
            hm32 = ins("vector", lambda E: E.tensor_tensor(h32_sb[:, 0:MT], g_sb[:, 0:MT],
                                                           psA[:, 512:512 + MT], ALU.mult), vS)
            last_g_rd = hm32
            wait("vector", vS, hm32)
            hs1 = ins("vector", lambda E, hg=hg, mv=mv: E.tensor_copy(
                hh_sb[:, hg * TS + mv * MT: hg * TS + (mv + 1) * MT], h32_sb[:, 0:MT]), vS)
            wait("vector", vS, hs1)
            hs2 = ins("vector", lambda E, hg=hg, mv=mv: E.tensor_tensor(
                hlo_sb[:, hg * TS + mv * MT: hg * TS + (mv + 1) * MT], h32_sb[:, 0:MT],
                hh_sb[:, hg * TS + mv * MT: hg * TS + (mv + 1) * MT], ALU.subtract), vS)
            last_hm = hs2
            h_ready[(hg, mv)] = hs2

        if hg == HGRP - 1:
            wait("tensor", vS, last_hm)
            for dc in range(NDC):
                for mv in range(NMT):
                    db = (dc * NMT + mv) % 2
                    psDn = psE if db == 0 else psF
                    wait("tensor", vS, last_psd_rd[db])
                    for g in range(HGRP):
                        hcg = hc - (HGRP - 1) + g
                        wslg = hcg % 2
                        mm3(lambda psDn=psDn: psDn[:, 0:MT],
                            lambda wslg=wslg, dc=dc: wdh[wslg][:, dc * 128:(dc + 1) * 128],
                            lambda wslg=wslg, dc=dc: wdl[wslg][:, dc * 128:(dc + 1) * 128],
                            lambda g=g, mv=mv: hh_sb[:, g * TS + mv * MT: g * TS + (mv + 1) * MT],
                            lambda g=g, mv=mv: hlo_sb[:, g * TS + mv * MT: g * TS + (mv + 1) * MT],
                            start=(g == 0), stop=(g == HGRP - 1), ctr=tS if g == HGRP - 1 else None)
                    dm = tS.n
                    last_wd_rd[0] = dm
                    last_wd_rd[1] = dm
                    wait("vector", tS, dm)
                    if hc < HGRP:
                        av = ins("vector", lambda E, dc=dc, mv=mv, psDn=psDn: E.tensor_copy(
                            acc[:, dc * OWN + mv * MT: dc * OWN + mv * MT + MT], psDn[:, 0:MT]), vS)
                    else:
                        wait("vector", vS, acc_written[(dc, mv)])
                        av = ins("vector", lambda E, dc=dc, mv=mv, psDn=psDn: E.tensor_tensor(
                            acc[:, dc * OWN + mv * MT: dc * OWN + mv * MT + MT],
                            acc[:, dc * OWN + mv * MT: dc * OWN + mv * MT + MT],
                            psDn[:, 0:MT], ALU.add), vS)
                    acc_written[(dc, mv)] = av
                    last_psd_rd[db] = av
            h_rd = tS.n
    barrier()
    esF.close()

    es4 = contextlib.ExitStack()
    oh_sb = es4.enter_context(nc.sbuf_tensor("oh_sb", [128, NDC * OWN], BF16))
    ol_sb = es4.enter_context(nc.sbuf_tensor("ol_sb", [128, NDC * OWN], BF16))
    o32_sb = es4.enter_context(nc.sbuf_tensor("o32_sb", [128, 4 * OWN], F32))
    wouth = [es4.enter_context(nc.sbuf_tensor(f"wouth{i}", [128, D], BF16)) for i in range(2)]
    woutl = [es4.enter_context(nc.sbuf_tensor(f"woutl{i}", [128, D], BF16)) for i in range(2)]

    wait("gpsimd", cc, 1)
    o_splits = {}
    for eb in range(4):
        wait("gpsimd", vS, o_splits.get(eb - 1, 0))
        def dynload(E, eb=eb):
            if eb == 0:
                E.reg_mov(dynregs["lin"], dynregs["coff"])
            r = E.dma_start(out=o32_sb[:].rearrange("p (b n) -> p b n", b=4),
                            in_=bass.AP(gout, dynregs["lin"], [[S, 128], [128 * S, 4], [1, OWN]]))
            E.reg_add(dynregs["lin"], dynregs["lin"], 4 * 128 * S)
            return r
        gl = ins("gpsimd", dynload, dR)
        wait("vector", dR, gl)
        s2 = 0
        for i in range(4):
            ec = eb * 4 + i
            s1 = ins("vector", lambda E, ec=ec, i=i: E.tensor_copy(
                oh_sb[:, ec * OWN:(ec + 1) * OWN], o32_sb[:, i * OWN:(i + 1) * OWN]), vS)
            wait("vector", vS, s1)
            s2 = ins("vector", lambda E, ec=ec, i=i: E.tensor_tensor(
                ol_sb[:, ec * OWN:(ec + 1) * OWN], o32_sb[:, i * OWN:(i + 1) * OWN],
                oh_sb[:, ec * OWN:(ec + 1) * OWN], ALU.subtract), vS)
        o_splits[eb] = s2
    o_ready = vS.n

    last_wo_rd = [0, 0]
    last_psP_rd4 = [0, 0]
    for dc in range(NDC):
        wsl = dc % 2
        wait("sync", vS, last_stage_rd[wsl])
        wold = ins("sync", lambda E, dc=dc, wsl=wsl: E.dma_start(
            out=wstage[wsl][:], in_=woutt[dc:dc + 1].rearrange("o p d -> (o p) d")), dW[wsl])
        wait("vector", dW[wsl], wold)
        wait("vector", tS, last_wo_rd[wsl])
        w1 = ins("vector", lambda E, wsl=wsl: E.tensor_copy(wouth[wsl][:], wstage[wsl][:]), vS)
        wait("vector", vS, w1)
        w2 = ins("vector", lambda E, wsl=wsl: E.tensor_tensor(
            woutl[wsl][:], wstage[wsl][:], wouth[wsl][:], ALU.subtract), vS)
        last_stage_rd[wsl] = w2
        for mv in range(NMT):
            pb = (dc * NMT + mv) % 2
            psP = psE if pb == 0 else psF
            wait("tensor", vS, w2)
            wait("tensor", vS, o_ready)
            wait("tensor", vS, last_psP_rd4[pb])
            for ec in range(NDC):
                mm3(lambda psP=psP: psP[:, 0:MT],
                    lambda wsl=wsl, ec=ec: wouth[wsl][:, ec * 128:(ec + 1) * 128],
                    lambda wsl=wsl, ec=ec: woutl[wsl][:, ec * 128:(ec + 1) * 128],
                    lambda ec=ec, mv=mv: oh_sb[:, ec * OWN + mv * MT: ec * OWN + mv * MT + MT],
                    lambda ec=ec, mv=mv: ol_sb[:, ec * OWN + mv * MT: ec * OWN + mv * MT + MT],
                    start=(ec == 0), stop=(ec == NDC - 1), ctr=tS if ec == NDC - 1 else None)
            mmn = tS.n
            last_wo_rd[wsl] = mmn
            wait("vector", tS, mmn)
            wait("vector", vS, acc_written[(dc, mv)])
            av = ins("vector", lambda E, dc=dc, mv=mv, psP=psP: E.tensor_tensor(
                acc[:, dc * OWN + mv * MT: dc * OWN + mv * MT + MT],
                acc[:, dc * OWN + mv * MT: dc * OWN + mv * MT + MT],
                psP[:, 0:MT], ALU.add), vS)
            acc_written[(dc, mv)] = av
            last_psP_rd4[pb] = av
    barrier()
    es4.close()

    xres = [es5.enter_context(nc.sbuf_tensor(f"xres{i}", [128, D], F32)) for i in range(2)]
    outb = [es5.enter_context(nc.sbuf_tensor(f"outb{i}", [128, D], F32)) for i in range(2)]
    last_out_store = [0, 0]
    last_xres_rd = [0, 0]
    last_pst_rd = [0, 0]
    for qs in range(OWN // 128):
        sl = qs % 2
        wait("sync", vS, last_xres_rd[sl])
        xl_ = ins("sync", lambda E, qs=qs, sl=sl: E.dma_start(
            out=xres[sl][:], in_=x[qs * 128:(qs + 1) * 128, :]), dX[sl])
        for dc in range(NDC):
            bk = dc % 2
            pst = psC if bk == 0 else psD
            wait("tensor", vS, last_pst_rd[bk])
            tt = ins("tensor", lambda E, dc=dc, qs=qs, pst=pst: E.transpose(
                pst[:, 0:128], acc[:, dc * OWN + qs * 128: dc * OWN + (qs + 1) * 128], id_sb[:]), tS)
            wait("vector", tS, tt)
            wait("vector", dX[sl], xl_)
            if dc == 0:
                wait("vector", dG[sl], last_out_store[sl])
            ad = ins("vector", lambda E, dc=dc, sl=sl, pst=pst: E.tensor_tensor(
                outb[sl][:, dc * 128:(dc + 1) * 128], pst[:, 0:128],
                xres[sl][:, dc * 128:(dc + 1) * 128], ALU.add), vS)
            last_pst_rd[bk] = ad
            if dc == NDC - 1:
                last_xres_rd[sl] = ad
        wait("gpsimd", vS, vS.n)
        st = ins("gpsimd", lambda E, qs=qs, sl=sl: E.dma_start(
            out=out[qs * 128:(qs + 1) * 128, :], in_=outb[sl][:]), dG[sl])
        last_out_store[sl] = st

    for c in ALL_DMA:
        wait("gpsimd", c)

    with nc.Block() as block:
        @block.sync
        def _(E):
            for f in prog["sync"]:
                f(E)

        @block.tensor
        def _(E):
            for f in prog["tensor"]:
                f(E)

        @block.vector
        def _(E):
            for f in prog["vector"]:
                f(E)

        @block.scalar
        def _(E):
            for f in prog["scalar"]:
                f(E)

        @block.gpsimd
        def _(E):
            for f in prog["gpsimd"]:
                f(E)

    es5.close()
    es.close()
    return nc


def _retile(wT, ng):
    d_in = wT.shape[0]
    ndc = d_in // 128
    return np.ascontiguousarray(
        wT.reshape(ndc, 128, ng, 128).transpose(2, 1, 0, 3).reshape(ng, 128, d_in))


def prepare_in_maps(x2d, Wqkv, Wout, w_gate, w_up, w_down, S):
    wqkvt = _retile(np.ascontiguousarray(Wqkv.T), 48)
    woutt = _retile(np.ascontiguousarray(Wout.T), 16)
    wgt = _retile(np.ascontiguousarray(w_gate.T), 64)
    wut = _retile(np.ascontiguousarray(w_up.T), 64)
    wdt = np.ascontiguousarray(w_down.T)
    ident = np.eye(128, dtype=np.float32)
    in_maps = []
    for c in range(8):
        xc = np.ascontiguousarray(np.roll(x2d, -(S // 8) * c, axis=0))
        slabs = np.ascontiguousarray(wqkvt[[16 + 2 * c, 16 + 2 * c + 1,
                                            32 + 2 * c, 32 + 2 * c + 1,
                                            2 * c, 2 * c + 1]])
        in_maps.append({
            "x": xc, "wqkvt6": slabs, "woutt": woutt,
            "wgt": wgt, "wut": wut, "wdt": wdt, "ident": ident,
            "coff": np.array([[(S // 8) * c]], dtype=np.int32),
        })
    return in_maps


def kernel(L0_attn_x, L0_attn_Wqkv, L0_attn_Wout, L0_ffn_w_gate, L0_ffn_w_up, L0_ffn_w_down):
    S = 8192
    x2d = np.ascontiguousarray(np.asarray(L0_attn_x, dtype=np.float32).reshape(S, D))
    in_maps = prepare_in_maps(
        x2d, np.asarray(L0_attn_Wqkv, dtype=np.float32),
        np.asarray(L0_attn_Wout, dtype=np.float32),
        np.asarray(L0_ffn_w_gate, dtype=np.float32),
        np.asarray(L0_ffn_w_up, dtype=np.float32),
        np.asarray(L0_ffn_w_down, dtype=np.float32), S)
    nc = build(S)
    res = run_bass_kernel_spmd(nc, in_maps, core_ids=list(range(8)))
    full = np.concatenate([res.results[c]["out"] for c in range(8)], axis=0)
    return full.reshape(1, S, D).astype(np.float32)
